# version 44
# baseline (speedup 1.0000x reference)
import numpy as np
from contextlib import ExitStack

import concourse.bass as bass
import concourse.mybir as mybir
from concourse.bass_utils import run_bass_kernel_spmd

F32 = mybir.dt.float32
BF16 = mybir.dt.bfloat16
AF = mybir.ActivationFunctionType
ALU = mybir.AluOpType
AX = mybir.AxisListType

S = 2048
D = 1024
NCH = 8
H = 8
GTOK = 256
NG = S // GTOK
NSUB = 2
SCALE = float(D // H) ** -0.5
EPS = 1e-5
ENGINES = ("pe", "act", "dve", "sp", "gp", "ra",
           "sp2", "sp3", "gp2", "gp3")
RING_POOL = {"sp": ("sp", "sp2", "sp3"), "gp": ("gp", "gp2", "gp3")}
PARANOID = False
DEBUG_DUMP = False


class Buf:
    __slots__ = ("w", "readers", "waw")

    def __init__(self, waw=False):
        self.w = None
        self.readers = {}
        self.waw = waw


class Plan:
    def __init__(self):
        self.ops = {e: [] for e in ENGINES}
        self.cnt = {e: 0 for e in ENGINES}
        self.floor = {e: {} for e in ENGINES}
        self.flushed = {e: 0 for e in ENGINES}
        self.ringn = {}

    def stage(self, eng, fn, reads=(), writes=(), inc=None, sem=None):
        if sem is None:
            if eng in RING_POOL:
                pool = RING_POOL[eng]
                self.ringn.setdefault(eng, 0)
                sem = pool[self.ringn[eng] % len(pool)]
                self.ringn[eng] += 1
            else:
                sem = eng
        if inc is None:
            inc = 16 if sem not in ("pe", "act", "dve") else 1
        waits = {}

        def need(tag, selfwait=False):
            if tag is None:
                return
            e2, v = tag
            if e2 == sem:
                if selfwait and eng in ("act", "dve") and v > self.flushed[eng]:
                    waits[eng] = max(waits.get(eng, 0), v)
                    self.flushed[eng] = v
                return
            if self.floor[eng].get(e2, 0) >= v:
                return
            waits[e2] = max(waits.get(e2, 0), v)

        if PARANOID:
            for e2 in ENGINES:
                if e2 != eng:
                    need((e2, self.cnt[e2]))
        for b in reads:
            need(b.w, selfwait=True)
        for b in writes:
            need(b.w, selfwait=b.waw)
            for e2, v in b.readers.items():
                need((e2, v), selfwait=True)
        for e2, v in waits.items():
            if e2 != sem:
                self.floor[eng][e2] = v
        if sem not in ("pe", "act", "dve") and self.cnt[sem] > 0:
            waits2 = dict(waits)
            waits2[sem] = self.cnt[sem]
            waits = waits2
        self.cnt[sem] += inc
        tag = (sem, self.cnt[sem])
        for b in reads:
            b.readers[sem] = self.cnt[sem]
        for b in writes:
            b.w = tag
            b.readers = {}
        self.ops[eng].append((sorted(waits.items()), fn, inc, sem))
        return tag

    def wait_only(self, eng):
        waits = [(e2, self.cnt[e2]) for e2 in ENGINES
                 if e2 != eng and self.cnt[e2] > self.floor[eng].get(e2, 0)]
        for e2, v in waits:
            self.floor[eng][e2] = v
        self.ops[eng].append((waits, None, 0, eng))

    def run(self, eng, handle, sems):
        for waits, fn, inc, sem in self.ops[eng]:
            for e2, v in waits:
                handle.wait_ge(sems[e2], v)
            if fn is not None:
                last = fn(handle)
                if inc:
                    last.then_inc(sems[sem], inc)


def build_program():
    nc = bass.Bass("TRN2", target_bir_lowering=False)

    x_h = nc.dram_tensor("x", [S, D], F32, kind="ExternalInput").ap()
    wq_h = nc.dram_tensor("wq", [D, D], F32, kind="ExternalInput").ap()
    wk_h = nc.dram_tensor("wk", [D, D], F32, kind="ExternalInput").ap()
    wv_h = nc.dram_tensor("wv", [D, D], F32, kind="ExternalInput").ap()
    wo_h = nc.dram_tensor("wo", [D, D], F32, kind="ExternalInput").ap()
    wf_h = nc.dram_tensor("wf", [D, D], F32, kind="ExternalInput").ap()
    bq_h = nc.dram_tensor("bq", [D], F32, kind="ExternalInput").ap()
    bk_h = nc.dram_tensor("bk", [D], F32, kind="ExternalInput").ap()
    bv_h = nc.dram_tensor("bv", [D], F32, kind="ExternalInput").ap()
    bo_h = nc.dram_tensor("bo", [D], F32, kind="ExternalInput").ap()
    y_h = nc.dram_tensor("y", [S, D], F32, kind="ExternalOutput").ap()
    mscr_h = nc.dram_tensor("mask_scratch", [1, 1280], BF16, kind="Internal").ap()
    dbg_h = {}
    if DEBUG_DUMP:
        for nm, dt_ in (("x_tm", F32), ("x_bf", BF16), ("x_fm", BF16),
                        ("q_fm", BF16), ("k_fm", BF16), ("v_tm", BF16),
                        ("expb", F32), ("probsn", BF16), ("probsT", BF16),
                        ("ao", BF16), ("r1", F32), ("ln1", F32),
                        ("ln1bf", BF16), ("ln1fm", BF16),
                        ("ssum", F32), ("rden", F32)):
            w = 4096 if nm in ("x_tm", "x_bf") else (16 if nm in ("ssum", "rden") else 2048)
            dbg_h[nm] = nc.dram_tensor("dbg_" + nm, [128, w], dt_,
                                       kind="ExternalOutput").ap()

    with ExitStack() as ctx:
        sb = lambda name, shape, dt: ctx.enter_context(nc.sbuf_tensor(name, shape, dt))
        ps = lambda name, shape: ctx.enter_context(nc.psum_tensor(name, shape, F32))

        wq = sb("wq_sb", [128, NCH * D], BF16)
        wk = sb("wk_sb", [128, NCH * D], BF16)
        wv = sb("wv_sb", [128, NCH * D], BF16)
        wo = sb("wo_sb", [128, NCH * D], BF16)
        swt = sb("swt_sb", [128, NCH * D], BF16)
        sws = sb("sws_sb", [128, NCH * D], BF16)
        brt = sb("brt", [128, 128], F32)
        bcol = sb("bcol", [128, 128], F32)
        bvc = sb("bv_bf", [128, NCH], BF16)
        borow = sb("bo_row", [1, D], F32)
        boeff = sb("boeff", [1, D], BF16)
        onesb = sb("ones_bf", [1, 256], BF16)
        masku = sb("masku", [2, 128], BF16)
        maskv = sb("maskv", [2, 512], BF16)

        x_tm = sb("x_tm", [128, 4096], F32)
        x_bf = sb("x_bf", [128, 4096], BF16)
        x_fm = sb("x_fm", [128, 4096], BF16)
        q_fm = sb("q_fm", [128, 2048], BF16)
        k_fm = sb("k_fm", [128, 2048], BF16)
        v_tm = sb("v_tm", [128, 2048], BF16)
        probsn = sb("probsn", [128, 2048], BF16)
        probsT = sb("probsT", [128, 2048], BF16)
        ao = sb("ao_sb", [128, 2048], BF16)
        r1 = sb("r1", [128, 2048], F32)
        ln1 = sb("ln1", [128, 2048], F32)
        ln1bf = sb("ln1bf", [128, 2048], BF16)
        ln1fm = sb("ln1fm", [128, 2048], BF16)
        outb = sb("outb", [128, 4096], F32)
        sqs = sb("sqs", [128, 1024], BF16)
        epsc = sb("epsc", [128, 1], F32)

        ssum = sb("ssum", [128, 2 * H], F32)
        rden = sb("rden", [128, 2 * H], F32)
        rsum1 = sb("rsum1", [128, 4], F32)
        ssq1 = sb("ssq1", [128, 2], F32)
        msum1 = sb("msum1", [128, 2], F32)
        msq1 = sb("msq1", [128, 4], F32)
        varr1 = sb("varr1", [128, 2], F32)
        lnv1 = sb("lnv1", [128, 2], F32)
        istd1 = sb("istd1", [128, 2], F32)
        ngm1 = sb("ngm1", [128, 2], F32)
        rsum2 = sb("rsum2", [128, 4], F32)
        ssq2 = sb("ssq2", [128, 2], F32)
        msum2 = sb("msum2", [128, 2], F32)
        msq2 = sb("msq2", [128, 4], F32)
        varr2 = sb("varr2", [128, 2], F32)
        lnv2 = sb("lnv2", [128, 2], F32)
        istd2 = sb("istd2", [128, 2], F32)
        ngm2 = sb("ngm2", [128, 2], F32)

        pb = [ps(f"pb{i}", [128, 512]) for i in range(6)]
        psc = ps("psc", [128, 1024])

        sems = {e: ctx.enter_context(nc.semaphore(f"sem_{e}")) for e in ENGINES}

        P = Plan()
        Bwq, Bwk, Bwv, Bwo, Bswt, Bsws = (Buf() for _ in range(6))
        Bsg = [Buf() for _ in range(NCH)]
        Bbq, Bbk, Bbv, Bborow, Bboeff, Bones = (Buf() for _ in range(6))
        Bbrt, Bbcol, Bbvc = Buf(), Buf(), Buf()
        Beps = Buf()
        Bmask = Buf()
        Bxtm = [Buf(), Buf()]
        Bxbf = [Buf(), Buf()]
        Boutb = [Buf(), Buf()]
        Bqfm, Bkfm, Bln1fm = (Buf() for _ in range(3))
        Bsqs = Buf(waw=True)
        Bxfm = [Buf(), Buf()]
        Bvtm = [Buf(), Buf()]
        Bprobsn = [Buf(), Buf()]
        BprobsT = [Buf(), Buf()]
        Bao = [Buf(), Buf()]
        Br = [Buf(), Buf()]
        Bln1 = [Buf(), Buf()]
        Bln1bf = [Buf(), Buf()]
        Bssum, Brden = Buf(), Buf()
        Brs1, Bsq1, Bms1, Bmq1, Bvr1, Blv1, Bis1, Bng1 = (Buf() for _ in range(8))
        Brs2, Bsq2, Bms2, Bmq2, Bvr2, Blv2, Bis2, Bng2 = (Buf() for _ in range(8))
        Bpb = [Buf() for _ in range(6)]
        Bpsc = Buf()

        rot = [0, 0]

        def next_bank(pool):
            if pool == 0:
                i = rot[0] % 4
            else:
                i = 4 + rot[1] % 2
            rot[pool] += 1
            return pb[i], Bpb[i]

        def ld_x(g, slot, out_t, cast):
            def fn(e, g=g, slot=slot, out_t=out_t):
                return e.dma_start(
                    out_t[:, slot * 2048:(slot + 1) * 2048].rearrange(
                        "p (s d) -> p s d", s=2),
                    x_h[g * GTOK:(g + 1) * GTOK, :].rearrange(
                        "(s p) d -> p s d", p=128),
                )
            return fn

        def ld_w(wt, wh, Bw):
            def fn(e, wt=wt, wh=wh):
                return e.dma_start(
                    wt[:, :].rearrange("p (k o) -> p k o", k=8),
                    wh.rearrange("(k p) o -> p k o", p=128))
            P.stage("gp", fn, writes=[Bw])

        def ld_b(bh, base, Bb):
            def fn(e, bh=bh, base=base):
                return e.dma_start(
                    brt[base:base + 8, :], bh.rearrange("(p f) -> p f", p=8))
            P.stage("sp", fn, reads=[Bbrt], writes=[Bb])

        P.stage("gp", ld_x(0, 0, x_bf, True), writes=[Bxbf[0]])
        ld_w(wq, wq_h, Bwq)
        P.stage("dve", lambda e: e.memset(brt[:, :], 0.0), writes=[Bbrt])
        ld_b(bq_h, 0, Bbq)
        ld_b(bk_h, 32, Bbk)
        ld_b(bv_h, 64, Bbv)
        for rb, Bb in ((0, Bbq), (32, Bbk), (64, Bbv)):
            for J in range(4):
                P.stage("dve", lambda e, rb=rb, J=J: e.transpose(
                    bcol[32 * J:32 * J + 32, rb:rb + 32],
                    brt[rb:rb + 32, 32 * J:32 * J + 32]),
                    reads=[Bb, Bbrt], writes=[Bbcol])
        P.stage("dve", lambda e: e.tensor_copy(bvc[:, :], bcol[:, 64:64 + NCH]),
                reads=[Bbcol], writes=[Bbvc])
        ld_w(wk, wk_h, Bwk)
        P.stage("gp", lambda e: e.dma_start(
            borow[:, :], bo_h.rearrange("(one d) -> one d", one=1)), writes=[Bborow])
        ld_w(wv, wv_h, Bwv)
        P.stage("gp", ld_x(0, 0, x_tm, False), writes=[Bxtm[0]])
        ld_w(wo, wo_h, Bwo)

        P.stage("dve", lambda e: e.memset(onesb[:, :], 1.0), writes=[Bones])

        Bmrow = Buf()
        mrow = ln1bf

        def mkmrow(e):
            e.memset(mrow[0:1, 0:64], 1.0)
            e.memset(mrow[0:1, 64:192], 0.0)
            e.memset(mrow[0:1, 192:256], 1.0)
            last = None
            for b in range(4):
                base = 256 + b * 128
                e.memset(mrow[0:1, base:base + 64], 0.0)
                e.memset(mrow[0:1, base + 64:base + 128], -30000.0)
                base = 768 + b * 128
                e.memset(mrow[0:1, base:base + 64], -30000.0)
                last = e.memset(mrow[0:1, base + 64:base + 128], 0.0)
            return last
        P.stage("dve", mkmrow, writes=[Bmrow])
        Bmscr = Buf()
        P.stage("sp", lambda e: e.dma_start(mscr_h, mrow[0:1, 0:1280]),
                reads=[Bmrow], writes=[Bmscr])
        P.stage("gp", lambda e: e.dma_start(
            masku[:, :], bass.AP(mscr_h.tensor, 0, [[128, 2], [1, 128]])),
            reads=[Bmscr], writes=[Bmask])
        P.stage("gp", lambda e: e.dma_start(
            maskv[:, :],
            bass.AP(mscr_h.tensor, 256, [[512, 2], [1, 512]])),
            reads=[Bmscr], writes=[Bmask])
        P.stage("dve", lambda e: e.memset(epsc[:, :], EPS), writes=[Beps])
        P.stage("dve", lambda e: e.memset(probsn[:, :], 0.0),
                writes=[Bprobsn[0], Bprobsn[1]])

        P.stage("gp", lambda e: e.dma_start(
            sws[:, :].rearrange("p (k o) -> p k o", k=8),
            wf_h.rearrange("(k p) o -> p k o", p=128)), writes=[Bsws])

        def emit_sws_sign(lo, hi):
            for oc in range(lo, hi):
                P.stage("act", lambda e, oc=oc: e.activation(
                    sws[:, oc * D:(oc + 1) * D], sws[:, oc * D:(oc + 1) * D],
                    AF.Sign),
                    reads=[Bsws], writes=[Bsg[oc]])

        def emit_sws_trans():
            for oc in range(NCH):
                def fn(e, oc=oc):
                    return e.dma_start(
                        bass.AP(swt, oc * 128, [[NCH * D, 128], [D, 8], [1, 128]]),
                        sws[:, oc * D:(oc + 1) * D].rearrange(
                            "p (c i) -> p c i", c=8),
                        transpose=True)
                if oc % 2 == 0:
                    P.stage("sp", fn, reads=[Bsg[oc]], writes=[Bswt])
                else:
                    P.stage("act", fn, reads=[Bsg[oc]], writes=[Bswt], sem="ra")

        def emit_loads(g):
            if g >= NG:
                return
            ns = g % 2
            P.stage("gp", ld_x(g, ns, x_bf, True), writes=[Bxbf[ns]])
            P.stage("gp", ld_x(g, ns, x_tm, False), writes=[Bxtm[ns]])

        def emit_xfm(g):
            if g >= NG:
                return
            slot = g % 2
            for s in range(NSUB):
                def fn(e, s=s, slot=slot):
                    return e.dma_start(
                        bass.AP(x_fm, slot * 2048 + s * 128,
                                [[4096, 128], [256, 8], [1, 128]]),
                        x_bf[:, slot * 2048 + s * 1024: slot * 2048 + (s + 1) * 1024
                             ].rearrange("p (c f) -> p c f", c=8),
                        transpose=True)
                P.stage("sp", fn, reads=[Bxbf[slot]], writes=[Bxfm[slot]])

        def emit_qk(g, which):
            if g >= NG:
                return
            sel = ((wq, Bwq, 0, q_fm, Bqfm),
                   (wk, Bwk, 32, k_fm, Bkfm))
            for (wt, Bw, boff, dst, Bdst) in sel[which:which + 1]:
                for m2 in range(NCH // 2):
                    bank, Bbank = next_bank(0)
                    slot = g % 2
                    for mi in range(2):
                        m = m2 * 2 + mi
                        def mmfn(e, wt=wt, m=m, mi=mi, bank=bank, slot=slot):
                            last = None
                            for kk in range(NCH):
                                last = e.matmul(
                                    bank[:, mi * GTOK:(mi + 1) * GTOK],
                                    wt[:, kk * D + m * 128: kk * D + (m + 1) * 128],
                                    x_fm[:, slot * 2048 + kk * GTOK:
                                         slot * 2048 + (kk + 1) * GTOK],
                                    start=(kk == 0), stop=(kk == NCH - 1))
                            return last
                        P.stage("pe", mmfn, reads=[Bw, Bxfm[slot]], writes=[Bbank])
                    for mi in range(2):
                        m = m2 * 2 + mi
                        def evfn(e, m=m, mi=mi, bank=bank, boff=boff, dst=dst):
                            return e.activation(
                                dst[:, m * GTOK:(m + 1) * GTOK],
                                bank[:, mi * GTOK:(mi + 1) * GTOK],
                                AF.Identity,
                                bias=bcol[:, boff + m:boff + m + 1], scale=1.0)
                        P.stage("act", evfn, reads=[Bbank, Bbcol], writes=[Bdst])

        def emit_v(g):
            for s in range(NSUB):
                for hf in range(2):
                    bank, Bbank = next_bank(0)
                    slot = g % 2
                    def mmfn(e, s=s, hf=hf, bank=bank, slot=slot):
                        last = None
                        for kk in range(NCH):
                            last = e.matmul(
                                bank[:, 0:512],
                                x_fm[:, slot * 2048 + kk * GTOK + s * 128:
                                     slot * 2048 + kk * GTOK + (s + 1) * 128],
                                wv[:, kk * D + hf * 512: kk * D + (hf + 1) * 512],
                                start=(kk == 0), stop=(kk == NCH - 1))
                        return last
                    P.stage("pe", mmfn, reads=[Bwv, Bxfm[slot]], writes=[Bbank])

                    def evfn(e, s=s, hf=hf, bank=bank):
                        return e.copy(
                            v_tm[:, s * 1024 + hf * 512: s * 1024 + (hf + 1) * 512],
                            bank[:, 0:512])
                    P.stage("act", evfn, reads=[Bbank], writes=[Bvtm[s]])

        def emit_boeff():
            for hf in range(2):
                bank, Bbank = next_bank(1)
                def mmfn(e, hf=hf, bank=bank):
                    last = None
                    for kk in range(NCH):
                        last = e.matmul(
                            bank[0:1, 0:512],
                            bvc[:, kk:kk + 1],
                            wo[:, kk * D + hf * 512: kk * D + (hf + 1) * 512],
                            start=(kk == 0), stop=(kk == NCH - 1))
                    return last
                P.stage("pe", mmfn, reads=[Bbvc, Bwo], writes=[Bbank])

                def ttfn(e, hf=hf, bank=bank):
                    return e.tensor_tensor(
                        boeff[0:1, hf * 512:(hf + 1) * 512],
                        bank[0:1, 0:512],
                        borow[0:1, hf * 512:(hf + 1) * 512], ALU.add)
                P.stage("dve", ttfn, reads=[Bbank, Bborow], writes=[Bboeff])

        def emit_scores_s(g, s):
            def scfn(e, s=s):
                last = None
                for half in range(2):
                    for hh in range(4):
                        h = half * 4 + hh
                        e.matmul(
                            psc[:, h * 128:(h + 1) * 128],
                            q_fm[:, h * GTOK + s * 128: h * GTOK + (s + 1) * 128],
                            k_fm[:, h * GTOK + s * 128: h * GTOK + (s + 1) * 128],
                            start=(hh == 0), stop=False)
                    last = e.matmul(
                        psc[:, half * 512:(half + 1) * 512],
                        masku[:, :], maskv[:, :],
                        start=False, stop=True)
                return last
            P.stage("pe", scfn, reads=[Bqfm, Bkfm, Bmask], writes=[Bpsc])

        def emit_exp_s(g, s):
            def expfn(e, s=s):
                return e.activation(
                    probsn[:, s * 1024:(s + 1) * 1024], psc[:, :],
                    AF.Exp, scale=SCALE)
            P.stage("act", expfn, reads=[Bpsc], writes=[Bprobsn[s]])

        def emit_softmax_s(g, s):
            def sumfn(e, s=s):
                return e.reduce_sum(
                    ssum[:, s * H:(s + 1) * H],
                    probsn[:, s * 1024:(s + 1) * 1024].rearrange(
                        "p (h k) -> p h k", h=H),
                    axis=AX.X)
            P.stage("dve", sumfn, reads=[Bprobsn[s]], writes=[Bssum])
            P.stage("dve", lambda e, s=s: e.reciprocal(
                rden[:, s * H:(s + 1) * H], ssum[:, s * H:(s + 1) * H]),
                reads=[Bssum], writes=[Brden])

            def nrmfn(e, s=s):
                return e.scalar_tensor_tensor(
                    probsn[:, s * 1024:(s + 1) * 1024],
                    probsn[:, s * 1024:(s + 1) * 1024], 1.0,
                    bass.AP(rden, s * H, [[2 * H, 128], [1, 8], [0, 128]]),
                    op0=ALU.mult, op1=ALU.mult)
            P.stage("dve", nrmfn, reads=[Bprobsn[s], Brden], writes=[Bprobsn[s]])

            def ptfn(e, s=s):
                return e.dma_start(
                    probsT[:, s * 1024:(s + 1) * 1024].rearrange(
                        "p (h q) -> p h q", h=H),
                    probsn[:, s * 1024:(s + 1) * 1024].rearrange(
                        "p (h k) -> p h k", h=H),
                    transpose=True)
            P.stage("sp", ptfn, reads=[Bprobsn[s]], writes=[BprobsT[s]])

        def emit_attnout(g):
            for s in range(NSUB):
                for half in range(2):
                    bank, Bbank = next_bank(1)
                    def aofn(e, s=s, half=half, bank=bank):
                        last = None
                        for hh in range(4):
                            h = half * 4 + hh
                            last = e.matmul(
                                bank[:, hh * 128:(hh + 1) * 128],
                                v_tm[:, s * 1024 + h * 128: s * 1024 + (h + 1) * 128],
                                probsT[:, s * 1024 + h * 128: s * 1024 + (h + 1) * 128],
                                start=True, stop=True)
                        return last
                    P.stage("pe", aofn, reads=[Bvtm[s], BprobsT[s]], writes=[Bbank])

                    def aoev(e, s=s, half=half, bank=bank):
                        return e.copy(
                            ao[:, s * 1024 + half * 512: s * 1024 + (half + 1) * 512],
                            bank[:, 0:512])
                    P.stage("act", aoev, reads=[Bbank], writes=[Bao[s]])

        def emit_oproj_s(g, s):
            slot = g % 2
            if True:
                for hf in range(2):
                    bank, Bbank = next_bank(1)
                    def opfn(e, s=s, hf=hf, bank=bank):
                        for kk in range(NCH):
                            e.matmul(
                                bank[:, 0:512],
                                ao[:, s * 1024 + kk * 128: s * 1024 + (kk + 1) * 128],
                                wo[:, kk * D + hf * 512: kk * D + (hf + 1) * 512],
                                start=(kk == 0), stop=False)
                        return e.matmul(
                            bank[:, 0:512], onesb[0:1, 0:128],
                            boeff[0:1, hf * 512:(hf + 1) * 512],
                            start=False, stop=True)
                    P.stage("pe", opfn, reads=[Bao[s], Bwo, Bones, Bboeff],
                            writes=[Bbank])

                    def r1fn(e, s=s, hf=hf, bank=bank, slot=slot):
                        off = s * 1024 + hf * 512
                        return e.scalar_tensor_tensor(
                            r1[:, off:off + 512], bank[:, 0:512], 1.0,
                            x_tm[:, slot * 2048 + off: slot * 2048 + off + 512],
                            op0=ALU.mult, op1=ALU.add,
                            accum_out=rsum1[:, s * 2 + hf: s * 2 + hf + 1])
                    P.stage("dve", r1fn, reads=[Bbank, Bxtm[slot]],
                            writes=[Br[s], Brs1])

        def emit_oproj(g):
            for s in range(NSUB):
                emit_oproj_s(g, s)

        def emit_ln1(g):
            for s in range(NSUB):
                def sqfn(e, s=s):
                    return e.scalar_tensor_tensor(
                        sqs[:, :], r1[:, s * 1024:(s + 1) * 1024], 1.0,
                        r1[:, s * 1024:(s + 1) * 1024],
                        op0=ALU.mult, op1=ALU.mult,
                        accum_out=ssq1[:, s:s + 1])
                P.stage("dve", sqfn, reads=[Br[s]], writes=[Bsqs, Bsq1])
            P.stage("dve", lambda e: e.tensor_tensor(
                msum1[:, 0:2], bass.AP(rsum1, 0, [[4, 128], [2, 2]]),
                bass.AP(rsum1, 1, [[4, 128], [2, 2]]), ALU.add),
                reads=[Brs1], writes=[Bms1])
            P.stage("dve", lambda e: e.tensor_tensor(
                msq1[:, 0:2], msum1[:, 0:2], msum1[:, 0:2], ALU.mult),
                reads=[Bms1], writes=[Bmq1])
            P.stage("dve", lambda e: e.scalar_tensor_tensor(
                varr1[:, 0:2], msq1[:, 0:2], 1.0 / D, ssq1[:, 0:2],
                op0=ALU.mult, op1=ALU.subtract),
                reads=[Bsq1, Bmq1], writes=[Bvr1])
            P.stage("act", lambda e: e.activation(
                lnv1[:, 0:2], varr1[:, 0:2], AF.Ln,
                bias=epsc[:, 0:1], scale=-1.0 / D),
                reads=[Bvr1, Beps], writes=[Blv1])
            P.stage("act", lambda e: e.activation(
                istd1[:, 0:2], lnv1[:, 0:2], AF.Exp, scale=-0.5),
                reads=[Blv1], writes=[Bis1])
            P.stage("dve", lambda e: e.tensor_scalar(
                msq1[:, 2:4], msum1[:, 0:2], -1.0 / D, None, op0=ALU.mult),
                reads=[Bms1], writes=[Bmq1])
            P.stage("dve", lambda e: e.tensor_tensor(
                ngm1[:, 0:2], istd1[:, 0:2], msq1[:, 2:4], ALU.mult),
                reads=[Bis1, Bmq1], writes=[Bng1])
            for s in range(NSUB):
                P.stage("dve", lambda e, s=s: e.tensor_scalar(
                    ln1[:, s * 1024:(s + 1) * 1024],
                    r1[:, s * 1024:(s + 1) * 1024],
                    istd1[:, s:s + 1], ngm1[:, s:s + 1],
                    op0=ALU.mult, op1=ALU.add),
                    reads=[Br[s], Bis1, Bng1], writes=[Bln1[s]])
                P.stage("act", lambda e, s=s: e.activation(
                    ln1bf[:, s * 1024:(s + 1) * 1024],
                    r1[:, s * 1024:(s + 1) * 1024],
                    AF.Identity, bias=ngm1[:, s:s + 1],
                    scale=istd1[:, s:s + 1]),
                    reads=[Br[s], Bis1, Bng1], writes=[Bln1bf[s], Bmrow]
                    if g == 0 else [Bln1bf[s]])

                def lfmfn(e, s=s):
                    return e.dma_start(
                        bass.AP(ln1fm, s * 128, [[2048, 128], [256, 8], [1, 128]]),
                        ln1bf[:, s * 1024:(s + 1) * 1024].rearrange(
                            "p (c f) -> p c f", c=8),
                        transpose=True)
                P.stage("sp", lfmfn, reads=[Bln1bf[s]], writes=[Bln1fm])

        def emit_ffn_s(g, s):
            if True:
                for hf in range(2):
                    bank, Bbank = next_bank(1)
                    def fffn(e, s=s, hf=hf, bank=bank):
                        last = None
                        for kk in range(NCH):
                            last = e.matmul(
                                bank[:, 0:512],
                                ln1fm[:, kk * GTOK + s * 128: kk * GTOK + (s + 1) * 128],
                                swt[:, kk * D + hf * 512: kk * D + (hf + 1) * 512],
                                start=(kk == 0), stop=(kk == NCH - 1))
                        return last
                    P.stage("pe", fffn, reads=[Bln1fm, Bswt], writes=[Bbank])

                    def r2fn(e, s=s, hf=hf, bank=bank):
                        off = s * 1024 + hf * 512
                        return e.scalar_tensor_tensor(
                            r1[:, off:off + 512], bank[:, 0:512], 1.0,
                            ln1[:, off:off + 512],
                            op0=ALU.mult, op1=ALU.add,
                            accum_out=rsum2[:, s * 2 + hf: s * 2 + hf + 1])
                    P.stage("dve", r2fn, reads=[Bbank, Bln1[s]],
                            writes=[Br[s], Brs2])

        def emit_ffn(g):
            for s in range(NSUB):
                emit_ffn_s(g, s)

        def emit_ln1_s(g, s):
            def sqfn(e, s=s):
                return e.scalar_tensor_tensor(
                    sqs[:, :], r1[:, s * 1024:(s + 1) * 1024], 1.0,
                    r1[:, s * 1024:(s + 1) * 1024],
                    op0=ALU.mult, op1=ALU.mult,
                    accum_out=ssq1[:, s:s + 1])
            P.stage("dve", sqfn, reads=[Br[s]], writes=[Bsqs, Bsq1])
            P.stage("dve", lambda e, s=s: e.tensor_tensor(
                msum1[:, s:s + 1], rsum1[:, 2 * s:2 * s + 1],
                rsum1[:, 2 * s + 1:2 * s + 2], ALU.add),
                reads=[Brs1], writes=[Bms1])
            P.stage("dve", lambda e, s=s: e.tensor_tensor(
                msq1[:, s:s + 1], msum1[:, s:s + 1], msum1[:, s:s + 1],
                ALU.mult),
                reads=[Bms1], writes=[Bmq1])
            P.stage("dve", lambda e, s=s: e.scalar_tensor_tensor(
                varr1[:, s:s + 1], msq1[:, s:s + 1], 1.0 / D, ssq1[:, s:s + 1],
                op0=ALU.mult, op1=ALU.subtract),
                reads=[Bsq1, Bmq1], writes=[Bvr1])
            P.stage("act", lambda e, s=s: e.activation(
                lnv1[:, s:s + 1], varr1[:, s:s + 1], AF.Ln,
                bias=epsc[:, 0:1], scale=-1.0 / D),
                reads=[Bvr1, Beps], writes=[Blv1])
            P.stage("act", lambda e, s=s: e.activation(
                istd1[:, s:s + 1], lnv1[:, s:s + 1], AF.Exp, scale=-0.5),
                reads=[Blv1], writes=[Bis1])
            P.stage("dve", lambda e, s=s: e.tensor_scalar(
                msq1[:, 2 + s:3 + s], msum1[:, s:s + 1], -1.0 / D, None,
                op0=ALU.mult),
                reads=[Bms1], writes=[Bmq1])
            P.stage("dve", lambda e, s=s: e.tensor_tensor(
                ngm1[:, s:s + 1], istd1[:, s:s + 1], msq1[:, 2 + s:3 + s],
                ALU.mult),
                reads=[Bis1, Bmq1], writes=[Bng1])
            P.stage("dve", lambda e, s=s: e.tensor_scalar(
                ln1[:, s * 1024:(s + 1) * 1024],
                r1[:, s * 1024:(s + 1) * 1024],
                istd1[:, s:s + 1], ngm1[:, s:s + 1],
                op0=ALU.mult, op1=ALU.add),
                reads=[Br[s], Bis1, Bng1], writes=[Bln1[s]])
            P.stage("act", lambda e, s=s: e.activation(
                ln1bf[:, s * 1024:(s + 1) * 1024],
                r1[:, s * 1024:(s + 1) * 1024],
                AF.Identity, bias=ngm1[:, s:s + 1],
                scale=istd1[:, s:s + 1]),
                reads=[Br[s], Bis1, Bng1], writes=[Bln1bf[s]])

            def lfmfn(e, s=s):
                return e.dma_start(
                    bass.AP(ln1fm, s * 128, [[2048, 128], [256, 8], [1, 128]]),
                    ln1bf[:, s * 1024:(s + 1) * 1024].rearrange(
                        "p (c f) -> p c f", c=8),
                    transpose=True)
            P.stage("sp", lfmfn, reads=[Bln1bf[s]], writes=[Bln1fm])

        def emit_ln2(g):
            slot = g % 2
            for s in range(NSUB):
                def sq2fn(e, s=s):
                    return e.scalar_tensor_tensor(
                        sqs[:, :], r1[:, s * 1024:(s + 1) * 1024], 1.0,
                        r1[:, s * 1024:(s + 1) * 1024],
                        op0=ALU.mult, op1=ALU.mult,
                        accum_out=ssq2[:, s:s + 1])
                P.stage("dve", sq2fn, reads=[Br[s]], writes=[Bsqs, Bsq2])
            P.stage("dve", lambda e: e.tensor_tensor(
                msum2[:, 0:2], bass.AP(rsum2, 0, [[4, 128], [2, 2]]),
                bass.AP(rsum2, 1, [[4, 128], [2, 2]]), ALU.add),
                reads=[Brs2], writes=[Bms2])
            P.stage("dve", lambda e: e.tensor_tensor(
                msq2[:, 0:2], msum2[:, 0:2], msum2[:, 0:2], ALU.mult),
                reads=[Bms2], writes=[Bmq2])
            P.stage("dve", lambda e: e.scalar_tensor_tensor(
                varr2[:, 0:2], msq2[:, 0:2], 1.0 / D, ssq2[:, 0:2],
                op0=ALU.mult, op1=ALU.subtract),
                reads=[Bsq2, Bmq2], writes=[Bvr2])
            P.stage("act", lambda e: e.activation(
                lnv2[:, 0:2], varr2[:, 0:2], AF.Ln,
                bias=epsc[:, 0:1], scale=-1.0 / D),
                reads=[Bvr2, Beps], writes=[Blv2])
            P.stage("act", lambda e: e.activation(
                istd2[:, 0:2], lnv2[:, 0:2], AF.Exp, scale=-0.5),
                reads=[Blv2], writes=[Bis2])
            P.stage("dve", lambda e: e.tensor_scalar(
                msq2[:, 2:4], msum2[:, 0:2], -1.0 / D, None, op0=ALU.mult),
                reads=[Bms2], writes=[Bmq2])
            P.stage("dve", lambda e: e.tensor_tensor(
                ngm2[:, 0:2], istd2[:, 0:2], msq2[:, 2:4], ALU.mult),
                reads=[Bis2, Bmq2], writes=[Bng2])
            for s in range(NSUB):
                P.stage("dve", lambda e, s=s, slot=slot: e.tensor_scalar(
                    outb[:, slot * 2048 + s * 1024: slot * 2048 + (s + 1) * 1024],
                    r1[:, s * 1024:(s + 1) * 1024],
                    istd2[:, s:s + 1], ngm2[:, s:s + 1],
                    op0=ALU.mult, op1=ALU.add),
                    reads=[Br[s], Bis2, Bng2], writes=[Boutb[slot]])

        def emit_store(g):
            slot = g % 2
            def stfn2(e, g=g, slot=slot):
                return e.dma_start(
                    y_h[g * GTOK:(g + 1) * GTOK, :].rearrange(
                        "(s p) d -> p s d", p=128),
                    outb[:, slot * 2048:(slot + 1) * 2048].rearrange(
                        "p (s d) -> p s d", s=2))
            P.stage("gp", stfn2, reads=[Boutb[slot]])

        emit_xfm(0)
        emit_qk(0, 0)
        emit_qk(0, 1)
        emit_scores_s(0, 0)
        emit_exp_s(0, 0)
        emit_softmax_s(0, 0)
        emit_scores_s(0, 1)
        emit_exp_s(0, 1)
        emit_softmax_s(0, 1)
        for g in range(NG):
            emit_loads(g + 1)
            emit_xfm(g + 1)
            emit_v(g)
            if g == 0:
                emit_boeff()
            emit_attnout(g)
            if g > 0:
                emit_ln2(g - 1)
                emit_store(g - 1)
            if g == NG - 1:
                emit_oproj_s(g, 0)
                emit_ln1_s(g, 0)
                emit_oproj_s(g, 1)
                emit_ffn_s(g, 0)
                emit_ln1_s(g, 1)
                emit_ffn_s(g, 1)
                continue
            emit_oproj(g)
            if g == 0:
                emit_sws_sign(0, 4)
            emit_qk(g + 1, 0)
            if g == 0:
                emit_sws_sign(4, NCH)
            emit_ln1(g)
            emit_qk(g + 1, 1)
            if g == 0:
                emit_sws_trans()
            emit_scores_s(g + 1, 0)
            emit_exp_s(g + 1, 0)
            emit_softmax_s(g + 1, 0)
            emit_ffn(g)
            emit_scores_s(g + 1, 1)
            emit_exp_s(g + 1, 1)
            emit_softmax_s(g + 1, 1)
        slot = (NG - 1) % 2
        for s in range(NSUB):
            def sqln(e, s=s):
                return e.scalar_tensor_tensor(
                    sqs[:, :], r1[:, s * 1024:(s + 1) * 1024], 1.0,
                    r1[:, s * 1024:(s + 1) * 1024],
                    op0=ALU.mult, op1=ALU.mult,
                    accum_out=ssq2[:, s:s + 1])
            P.stage("dve", sqln, reads=[Br[s]], writes=[Bsqs, Bsq2])
            P.stage("dve", lambda e, s=s: e.tensor_tensor(
                msum2[:, s:s + 1], rsum2[:, 2 * s:2 * s + 1],
                rsum2[:, 2 * s + 1:2 * s + 2], ALU.add),
                reads=[Brs2], writes=[Bms2])
            P.stage("dve", lambda e, s=s: e.tensor_tensor(
                msq2[:, s:s + 1], msum2[:, s:s + 1], msum2[:, s:s + 1],
                ALU.mult),
                reads=[Bms2], writes=[Bmq2])
            P.stage("dve", lambda e, s=s: e.scalar_tensor_tensor(
                varr2[:, s:s + 1], msq2[:, s:s + 1], 1.0 / D, ssq2[:, s:s + 1],
                op0=ALU.mult, op1=ALU.subtract),
                reads=[Bsq2, Bmq2], writes=[Bvr2])
            P.stage("act", lambda e, s=s: e.activation(
                lnv2[:, s:s + 1], varr2[:, s:s + 1], AF.Ln,
                bias=epsc[:, 0:1], scale=-1.0 / D),
                reads=[Bvr2, Beps], writes=[Blv2])
            P.stage("act", lambda e, s=s: e.activation(
                istd2[:, s:s + 1], lnv2[:, s:s + 1], AF.Exp, scale=-0.5),
                reads=[Blv2], writes=[Bis2])
            P.stage("dve", lambda e, s=s: e.tensor_scalar(
                msq2[:, 2 + s:3 + s], msum2[:, s:s + 1], -1.0 / D, None,
                op0=ALU.mult),
                reads=[Bms2], writes=[Bmq2])
            P.stage("dve", lambda e, s=s: e.tensor_tensor(
                ngm2[:, s:s + 1], istd2[:, s:s + 1], msq2[:, 2 + s:3 + s],
                ALU.mult),
                reads=[Bis2, Bmq2], writes=[Bng2])
            P.stage("dve", lambda e, s=s, slot=slot: e.tensor_scalar(
                outb[:, slot * 2048 + s * 1024: slot * 2048 + (s + 1) * 1024],
                r1[:, s * 1024:(s + 1) * 1024],
                istd2[:, s:s + 1], ngm2[:, s:s + 1],
                op0=ALU.mult, op1=ALU.add),
                reads=[Br[s], Bis2, Bng2], writes=[Boutb[slot]])

            def stl(e, s=s, slot=slot):
                g = NG - 1
                return e.dma_start(
                    y_h[g * GTOK + s * 128: g * GTOK + (s + 1) * 128, :],
                    outb[:, slot * 2048 + s * 1024: slot * 2048 + (s + 1) * 1024])
            P.stage("gp", stl, reads=[Boutb[slot]])


        P.wait_only("sp")
        P.wait_only("gp")

        nums = sorted(s.num for s in sems.values())
        assert nums == list(range(nums[0], nums[0] + len(nums))), nums
        nc.gpsimd.sem_clear(range(nums[0], nums[-1] + 1))
        nc.all_engine_barrier()

        with nc.Block() as block:
            @block.tensor
            def _(eng):
                P.run("pe", eng, sems)

            @block.scalar
            def _(eng):
                P.run("act", eng, sems)

            @block.vector
            def _(eng):
                P.run("dve", eng, sems)

            @block.sync
            def _(eng):
                P.run("sp", eng, sems)

            @block.gpsimd
            def _(eng):
                P.run("gp", eng, sems)

    return nc


_NC_CACHE = []


def kernel(L0_attn_src, L0_attn_Wq, L0_attn_bq, L0_attn_Wk, L0_attn_bk,
           L0_attn_Wv, L0_attn_bv, L0_attn_Wo, L0_attn_bo, L0_ffn_weight):
    x = np.ascontiguousarray(np.asarray(L0_attn_src))
    ws = {
        "wq": np.ascontiguousarray(np.asarray(L0_attn_Wq)),
        "wk": np.ascontiguousarray(np.asarray(L0_attn_Wk)),
        "wv": np.ascontiguousarray(np.asarray(L0_attn_Wv)),
        "wo": np.ascontiguousarray(np.asarray(L0_attn_Wo)),
        "wf": np.ascontiguousarray(np.asarray(L0_ffn_weight)),
        "bq": np.ascontiguousarray(np.asarray(L0_attn_bq)),
        "bk": np.ascontiguousarray(np.asarray(L0_attn_bk)),
        "bv": np.ascontiguousarray(np.asarray(L0_attn_bv)),
        "bo": np.ascontiguousarray(np.asarray(L0_attn_bo)),
    }
    if not _NC_CACHE:
        _NC_CACHE.append(build_program())
    nc = _NC_CACHE[0]
    in_maps = [dict(ws, x=x[c]) for c in range(8)]
    res = run_bass_kernel_spmd(nc, in_maps, core_ids=list(range(8)))
    return np.stack([np.asarray(res.results[c]["y"]) for c in range(8)], axis=0)


# version 45
# speedup vs baseline: 1.0590x; 1.0590x over previous
import numpy as np
from contextlib import ExitStack

import concourse.bass as bass
import concourse.mybir as mybir
from concourse.bass_utils import run_bass_kernel_spmd

F32 = mybir.dt.float32
BF16 = mybir.dt.bfloat16
AF = mybir.ActivationFunctionType
ALU = mybir.AluOpType
AX = mybir.AxisListType

S = 2048
D = 1024
NCH = 8
H = 8
GTOK = 256
NG = S // GTOK
NSUB = 2
SCALE = float(D // H) ** -0.5
EPS = 1e-5
ENGINES = ("pe", "act", "dve", "sp", "gp", "ra",
           "sp2", "sp3", "gp2", "gp3", "ra2", "ra3")
RING_POOL = {"sp": ("sp", "sp2", "sp3"), "gp": ("gp", "gp2", "gp3"),
             "ra": ("ra", "ra2", "ra3")}
PARANOID = False
DEBUG_DUMP = False


class Buf:
    __slots__ = ("w", "readers", "waw")

    def __init__(self, waw=False):
        self.w = None
        self.readers = {}
        self.waw = waw


class Plan:
    def __init__(self):
        self.ops = {e: [] for e in ENGINES}
        self.cnt = {e: 0 for e in ENGINES}
        self.floor = {e: {} for e in ENGINES}
        self.flushed = {e: 0 for e in ENGINES}
        self.ringn = {}

    def stage(self, eng, fn, reads=(), writes=(), inc=None, sem=None):
        if sem is None:
            sem = eng
        if sem in RING_POOL:
            pool = RING_POOL[sem]
            self.ringn.setdefault(sem, 0)
            key = sem
            sem = pool[self.ringn[key] % len(pool)]
            self.ringn[key] += 1
        if inc is None:
            inc = 16 if sem not in ("pe", "act", "dve") else 1
        waits = {}

        def need(tag, selfwait=False):
            if tag is None:
                return
            e2, v = tag
            if e2 == sem:
                if selfwait and eng in ("act", "dve") and v > self.flushed[eng]:
                    waits[eng] = max(waits.get(eng, 0), v)
                    self.flushed[eng] = v
                return
            if self.floor[eng].get(e2, 0) >= v:
                return
            waits[e2] = max(waits.get(e2, 0), v)

        if PARANOID:
            for e2 in ENGINES:
                if e2 != eng:
                    need((e2, self.cnt[e2]))
        for b in reads:
            need(b.w, selfwait=True)
        for b in writes:
            need(b.w, selfwait=b.waw)
            for e2, v in b.readers.items():
                need((e2, v), selfwait=True)
        for e2, v in waits.items():
            if e2 != sem:
                self.floor[eng][e2] = v
        if sem not in ("pe", "act", "dve") and self.cnt[sem] > 0:
            waits2 = dict(waits)
            waits2[sem] = self.cnt[sem]
            waits = waits2
        self.cnt[sem] += inc
        tag = (sem, self.cnt[sem])
        for b in reads:
            b.readers[sem] = self.cnt[sem]
        for b in writes:
            b.w = tag
            b.readers = {}
        self.ops[eng].append((sorted(waits.items()), fn, inc, sem))
        return tag

    def wait_only(self, eng):
        waits = [(e2, self.cnt[e2]) for e2 in ENGINES
                 if e2 != eng and self.cnt[e2] > self.floor[eng].get(e2, 0)]
        for e2, v in waits:
            self.floor[eng][e2] = v
        self.ops[eng].append((waits, None, 0, eng))

    def run(self, eng, handle, sems):
        for waits, fn, inc, sem in self.ops[eng]:
            for e2, v in waits:
                handle.wait_ge(sems[e2], v)
            if fn is not None:
                last = fn(handle)
                if inc:
                    last.then_inc(sems[sem], inc)


def build_program():
    nc = bass.Bass("TRN2", target_bir_lowering=False)

    x_h = nc.dram_tensor("x", [S, D], F32, kind="ExternalInput").ap()
    wq_h = nc.dram_tensor("wq", [D, D], F32, kind="ExternalInput").ap()
    wk_h = nc.dram_tensor("wk", [D, D], F32, kind="ExternalInput").ap()
    wv_h = nc.dram_tensor("wv", [D, D], F32, kind="ExternalInput").ap()
    wo_h = nc.dram_tensor("wo", [D, D], F32, kind="ExternalInput").ap()
    wf_h = nc.dram_tensor("wf", [D, D], F32, kind="ExternalInput").ap()
    bq_h = nc.dram_tensor("bq", [D], F32, kind="ExternalInput").ap()
    bk_h = nc.dram_tensor("bk", [D], F32, kind="ExternalInput").ap()
    bv_h = nc.dram_tensor("bv", [D], F32, kind="ExternalInput").ap()
    bo_h = nc.dram_tensor("bo", [D], F32, kind="ExternalInput").ap()
    y_h = nc.dram_tensor("y", [S, D], F32, kind="ExternalOutput").ap()
    mscr_h = nc.dram_tensor("mask_scratch", [1, 1280], BF16, kind="Internal").ap()
    dbg_h = {}
    if DEBUG_DUMP:
        for nm, dt_ in (("x_tm", F32), ("x_bf", BF16), ("x_fm", BF16),
                        ("q_fm", BF16), ("k_fm", BF16), ("v_tm", BF16),
                        ("expb", F32), ("probsn", BF16), ("probsT", BF16),
                        ("ao", BF16), ("r1", F32), ("ln1", F32),
                        ("ln1bf", BF16), ("ln1fm", BF16),
                        ("ssum", F32), ("rden", F32)):
            w = 4096 if nm in ("x_tm", "x_bf") else (16 if nm in ("ssum", "rden") else 2048)
            dbg_h[nm] = nc.dram_tensor("dbg_" + nm, [128, w], dt_,
                                       kind="ExternalOutput").ap()

    with ExitStack() as ctx:
        sb = lambda name, shape, dt: ctx.enter_context(nc.sbuf_tensor(name, shape, dt))
        ps = lambda name, shape: ctx.enter_context(nc.psum_tensor(name, shape, F32))

        wq = sb("wq_sb", [128, NCH * D], BF16)
        wk = sb("wk_sb", [128, NCH * D], BF16)
        wv = sb("wv_sb", [128, NCH * D], BF16)
        wo = sb("wo_sb", [128, NCH * D], BF16)
        swt = sb("swt_sb", [128, NCH * D], BF16)
        sws = sb("sws_sb", [128, NCH * D], BF16)
        brt = sb("brt", [128, 128], F32)
        bcol = sb("bcol", [128, 128], F32)
        bvc = sb("bv_bf", [128, NCH], BF16)
        borow = sb("bo_row", [1, D], F32)
        boeff = sb("boeff", [1, D], BF16)
        onesb = sb("ones_bf", [1, 256], BF16)
        masku = sb("masku", [2, 128], BF16)
        maskv = sb("maskv", [2, 512], BF16)

        x_tm = sb("x_tm", [128, 4096], F32)
        x_bf = sb("x_bf", [128, 4096], BF16)
        x_fm = sb("x_fm", [128, 4096], BF16)
        q_fm = sb("q_fm", [128, 2048], BF16)
        k_fm = sb("k_fm", [128, 2048], BF16)
        v_tm = sb("v_tm", [128, 2048], BF16)
        probsn = sb("probsn", [128, 2048], BF16)
        probsT = sb("probsT", [128, 2048], BF16)
        ao = sb("ao_sb", [128, 2048], BF16)
        r1 = sb("r1", [128, 2048], F32)
        ln1 = sb("ln1", [128, 2048], F32)
        ln1bf = sb("ln1bf", [128, 2048], BF16)
        ln1fm = sb("ln1fm", [128, 2048], BF16)
        outb = sb("outb", [128, 4096], F32)
        sqs = sb("sqs", [128, 1024], BF16)
        epsc = sb("epsc", [128, 1], F32)

        ssum = sb("ssum", [128, 2 * H], F32)
        rden = sb("rden", [128, 2 * H], F32)
        rsum1 = sb("rsum1", [128, 4], F32)
        ssq1 = sb("ssq1", [128, 2], F32)
        msum1 = sb("msum1", [128, 2], F32)
        msq1 = sb("msq1", [128, 4], F32)
        varr1 = sb("varr1", [128, 2], F32)
        lnv1 = sb("lnv1", [128, 2], F32)
        istd1 = sb("istd1", [128, 2], F32)
        ngm1 = sb("ngm1", [128, 2], F32)
        rsum2 = sb("rsum2", [128, 4], F32)
        ssq2 = sb("ssq2", [128, 2], F32)
        msum2 = sb("msum2", [128, 2], F32)
        msq2 = sb("msq2", [128, 4], F32)
        varr2 = sb("varr2", [128, 2], F32)
        lnv2 = sb("lnv2", [128, 2], F32)
        istd2 = sb("istd2", [128, 2], F32)
        ngm2 = sb("ngm2", [128, 2], F32)

        pb = [ps(f"pb{i}", [128, 512]) for i in range(6)]
        psc = ps("psc", [128, 1024])

        sems = {e: ctx.enter_context(nc.semaphore(f"sem_{e}")) for e in ENGINES}

        P = Plan()
        Bwq, Bwk, Bwv, Bwo = (Buf() for _ in range(4))
        Bsws = [Buf() for _ in range(NCH)]
        Bsg = [Buf() for _ in range(NCH)]
        Bswt = [Buf() for _ in range(NCH)]
        Bbq, Bbk, Bbv, Bborow, Bboeff, Bones = (Buf() for _ in range(6))
        Bbrt, Bbcol, Bbvc = Buf(), Buf(), Buf()
        Beps = Buf()
        Bmask = Buf()
        Bxtm = [Buf(), Buf()]
        Bxbf = [Buf(), Buf()]
        Boutb = [Buf(), Buf()]
        Bqfm, Bkfm = Buf(), Buf()
        Bln1fm = [Buf(), Buf()]
        Bsqs = Buf(waw=True)
        Bxfm = [[Buf(), Buf()], [Buf(), Buf()]]
        Bvtm = [Buf(), Buf()]
        Bprobsn = [Buf(), Buf()]
        BprobsT = [Buf(), Buf()]
        Bao = [Buf(), Buf()]
        Br = [Buf(), Buf()]
        Bln1 = [Buf(), Buf()]
        Bln1bf = [Buf(), Buf()]
        Bssum, Brden = Buf(), Buf()
        Brs1, Bsq1, Bms1, Bmq1, Bvr1, Blv1, Bis1, Bng1 = (Buf() for _ in range(8))
        Brs2, Bsq2, Bms2, Bmq2, Bvr2, Blv2, Bis2, Bng2 = (Buf() for _ in range(8))
        Bpb = [Buf() for _ in range(6)]
        Bpsc = Buf()

        rot = [0, 0]

        def next_bank(pool):
            if pool == 0:
                i = rot[0] % 4
            else:
                i = 4 + rot[1] % 2
            rot[pool] += 1
            return pb[i], Bpb[i]

        def ld_x(g, slot, out_t, cast):
            def fn(e, g=g, slot=slot, out_t=out_t):
                return e.dma_start(
                    out_t[:, slot * 2048:(slot + 1) * 2048].rearrange(
                        "p (s d) -> p s d", s=2),
                    x_h[g * GTOK:(g + 1) * GTOK, :].rearrange(
                        "(s p) d -> p s d", p=128),
                )
            return fn

        def ld_w(wt, wh, Bw):
            def fn(e, wt=wt, wh=wh):
                return e.dma_start(
                    wt[:, :].rearrange("p (k o) -> p k o", k=8),
                    wh.rearrange("(k p) o -> p k o", p=128))
            P.stage("gp", fn, writes=[Bw])

        def ld_b(bh, base, Bb):
            def fn(e, bh=bh, base=base):
                return e.dma_start(
                    brt[base:base + 8, :], bh.rearrange("(p f) -> p f", p=8))
            P.stage("sp", fn, reads=[Bbrt], writes=[Bb])

        P.stage("gp", ld_x(0, 0, x_bf, True), writes=[Bxbf[0]])
        ld_w(wq, wq_h, Bwq)
        P.stage("dve", lambda e: e.memset(brt[:, :], 0.0), writes=[Bbrt])
        ld_b(bq_h, 0, Bbq)
        ld_b(bk_h, 32, Bbk)
        ld_b(bv_h, 64, Bbv)
        for rb, Bb in ((0, Bbq), (32, Bbk), (64, Bbv)):
            for J in range(4):
                P.stage("dve", lambda e, rb=rb, J=J: e.transpose(
                    bcol[32 * J:32 * J + 32, rb:rb + 32],
                    brt[rb:rb + 32, 32 * J:32 * J + 32]),
                    reads=[Bb, Bbrt], writes=[Bbcol])
        P.stage("dve", lambda e: e.tensor_copy(bvc[:, :], bcol[:, 64:64 + NCH]),
                reads=[Bbcol], writes=[Bbvc])
        ld_w(wk, wk_h, Bwk)
        P.stage("gp", lambda e: e.dma_start(
            borow[:, :], bo_h.rearrange("(one d) -> one d", one=1)), writes=[Bborow])
        ld_w(wv, wv_h, Bwv)
        P.stage("gp", ld_x(0, 0, x_tm, False), writes=[Bxtm[0]])
        ld_w(wo, wo_h, Bwo)

        P.stage("dve", lambda e: e.memset(onesb[:, :], 1.0), writes=[Bones])

        Bmrow = Buf()
        mrow = ln1bf

        def mkmrow(e):
            e.memset(mrow[0:1, 0:64], 1.0)
            e.memset(mrow[0:1, 64:192], 0.0)
            e.memset(mrow[0:1, 192:256], 1.0)
            last = None
            for b in range(4):
                base = 256 + b * 128
                e.memset(mrow[0:1, base:base + 64], 0.0)
                e.memset(mrow[0:1, base + 64:base + 128], -30000.0)
                base = 768 + b * 128
                e.memset(mrow[0:1, base:base + 64], -30000.0)
                last = e.memset(mrow[0:1, base + 64:base + 128], 0.0)
            return last
        P.stage("dve", mkmrow, writes=[Bmrow])
        Bmscr = Buf()
        P.stage("sp", lambda e: e.dma_start(mscr_h, mrow[0:1, 0:1280]),
                reads=[Bmrow], writes=[Bmscr])
        P.stage("gp", lambda e: e.dma_start(
            masku[:, :], bass.AP(mscr_h.tensor, 0, [[128, 2], [1, 128]])),
            reads=[Bmscr], writes=[Bmask])
        P.stage("gp", lambda e: e.dma_start(
            maskv[:, :],
            bass.AP(mscr_h.tensor, 256, [[512, 2], [1, 512]])),
            reads=[Bmscr], writes=[Bmask])
        P.stage("dve", lambda e: e.memset(epsc[:, :], EPS), writes=[Beps])
        P.stage("dve", lambda e: e.memset(probsn[:, :], 0.0),
                writes=[Bprobsn[0], Bprobsn[1]])

        for oc in range(NCH):
            P.stage("gp", lambda e, oc=oc: e.dma_start(
                sws[:, oc * D:(oc + 1) * D],
                wf_h[oc * 128:(oc + 1) * 128, :]), writes=[Bsws[oc]])

        def emit_sws_sign(lo, hi):
            for oc in range(lo, hi):
                P.stage("act", lambda e, oc=oc: e.activation(
                    sws[:, oc * D:(oc + 1) * D], sws[:, oc * D:(oc + 1) * D],
                    AF.Sign),
                    reads=[Bsws[oc]], writes=[Bsg[oc]])

        def emit_sws_trans():
            for oc in range(NCH):
                def fn(e, oc=oc):
                    return e.dma_start(
                        bass.AP(swt, oc * 128, [[NCH * D, 128], [D, 8], [1, 128]]),
                        sws[:, oc * D:(oc + 1) * D].rearrange(
                            "p (c i) -> p c i", c=8),
                        transpose=True)
                P.stage("sp", fn, reads=[Bsg[oc]], writes=[Bswt[oc]])

        def emit_loads(g):
            if g >= NG:
                return
            ns = g % 2
            P.stage("gp", ld_x(g, ns, x_bf, True), writes=[Bxbf[ns]])
            P.stage("gp", ld_x(g, ns, x_tm, False), writes=[Bxtm[ns]])

        def emit_xfm(g):
            if g >= NG:
                return
            slot = g % 2
            for s in range(NSUB):
                def fn(e, s=s, slot=slot):
                    return e.dma_start(
                        bass.AP(x_fm, slot * 2048 + s * 128,
                                [[4096, 128], [256, 8], [1, 128]]),
                        x_bf[:, slot * 2048 + s * 1024: slot * 2048 + (s + 1) * 1024
                             ].rearrange("p (c f) -> p c f", c=8),
                        transpose=True)
                P.stage("sp", fn, reads=[Bxbf[slot]], writes=[Bxfm[slot][s]])

        def emit_qk(g, which):
            if g >= NG:
                return
            sel = ((wq, Bwq, 0, q_fm, Bqfm),
                   (wk, Bwk, 32, k_fm, Bkfm))
            for (wt, Bw, boff, dst, Bdst) in sel[which:which + 1]:
                for m2 in range(NCH // 2):
                    bank, Bbank = next_bank(0)
                    slot = g % 2
                    for mi in range(2):
                        m = m2 * 2 + mi
                        def mmfn(e, wt=wt, m=m, mi=mi, bank=bank, slot=slot):
                            last = None
                            for kk in range(NCH):
                                last = e.matmul(
                                    bank[:, mi * GTOK:(mi + 1) * GTOK],
                                    wt[:, kk * D + m * 128: kk * D + (m + 1) * 128],
                                    x_fm[:, slot * 2048 + kk * GTOK:
                                         slot * 2048 + (kk + 1) * GTOK],
                                    start=(kk == 0), stop=(kk == NCH - 1))
                            return last
                        P.stage("pe", mmfn,
                                reads=[Bw, Bxfm[slot][0], Bxfm[slot][1]],
                                writes=[Bbank])
                    for mi in range(2):
                        m = m2 * 2 + mi
                        def evfn(e, m=m, mi=mi, bank=bank, boff=boff, dst=dst):
                            return e.activation(
                                dst[:, m * GTOK:(m + 1) * GTOK],
                                bank[:, mi * GTOK:(mi + 1) * GTOK],
                                AF.Identity,
                                bias=bcol[:, boff + m:boff + m + 1], scale=1.0)
                        P.stage("act", evfn, reads=[Bbank, Bbcol], writes=[Bdst])

        def emit_v(g):
            for s in range(NSUB):
                for hf in range(2):
                    bank, Bbank = next_bank(0)
                    slot = g % 2
                    def mmfn(e, s=s, hf=hf, bank=bank, slot=slot):
                        last = None
                        for kk in range(NCH):
                            last = e.matmul(
                                bank[:, 0:512],
                                x_fm[:, slot * 2048 + kk * GTOK + s * 128:
                                     slot * 2048 + kk * GTOK + (s + 1) * 128],
                                wv[:, kk * D + hf * 512: kk * D + (hf + 1) * 512],
                                start=(kk == 0), stop=(kk == NCH - 1))
                        return last
                    P.stage("pe", mmfn, reads=[Bwv, Bxfm[slot][s]], writes=[Bbank])

                    def evfn(e, s=s, hf=hf, bank=bank):
                        return e.copy(
                            v_tm[:, s * 1024 + hf * 512: s * 1024 + (hf + 1) * 512],
                            bank[:, 0:512])
                    P.stage("act", evfn, reads=[Bbank], writes=[Bvtm[s]])

        def emit_boeff():
            for hf in range(2):
                bank, Bbank = next_bank(1)
                def mmfn(e, hf=hf, bank=bank):
                    last = None
                    for kk in range(NCH):
                        last = e.matmul(
                            bank[0:1, 0:512],
                            bvc[:, kk:kk + 1],
                            wo[:, kk * D + hf * 512: kk * D + (hf + 1) * 512],
                            start=(kk == 0), stop=(kk == NCH - 1))
                    return last
                P.stage("pe", mmfn, reads=[Bbvc, Bwo], writes=[Bbank])

                def ttfn(e, hf=hf, bank=bank):
                    return e.tensor_tensor(
                        boeff[0:1, hf * 512:(hf + 1) * 512],
                        bank[0:1, 0:512],
                        borow[0:1, hf * 512:(hf + 1) * 512], ALU.add)
                P.stage("dve", ttfn, reads=[Bbank, Bborow], writes=[Bboeff])

        def emit_scores_s(g, s):
            def scfn(e, s=s):
                last = None
                for half in range(2):
                    for hh in range(4):
                        h = half * 4 + hh
                        e.matmul(
                            psc[:, h * 128:(h + 1) * 128],
                            q_fm[:, h * GTOK + s * 128: h * GTOK + (s + 1) * 128],
                            k_fm[:, h * GTOK + s * 128: h * GTOK + (s + 1) * 128],
                            start=(hh == 0), stop=False)
                    last = e.matmul(
                        psc[:, half * 512:(half + 1) * 512],
                        masku[:, :], maskv[:, :],
                        start=False, stop=True)
                return last
            P.stage("pe", scfn, reads=[Bqfm, Bkfm, Bmask], writes=[Bpsc])

        def emit_exp_s(g, s):
            def expfn(e, s=s):
                return e.activation(
                    probsn[:, s * 1024:(s + 1) * 1024], psc[:, :],
                    AF.Exp, scale=SCALE)
            P.stage("act", expfn, reads=[Bpsc], writes=[Bprobsn[s]])

        def emit_softmax_s(g, s):
            def sumfn(e, s=s):
                return e.reduce_sum(
                    ssum[:, s * H:(s + 1) * H],
                    probsn[:, s * 1024:(s + 1) * 1024].rearrange(
                        "p (h k) -> p h k", h=H),
                    axis=AX.X)
            P.stage("dve", sumfn, reads=[Bprobsn[s]], writes=[Bssum])
            P.stage("dve", lambda e, s=s: e.reciprocal(
                rden[:, s * H:(s + 1) * H], ssum[:, s * H:(s + 1) * H]),
                reads=[Bssum], writes=[Brden])

            def nrmfn(e, s=s):
                return e.scalar_tensor_tensor(
                    probsn[:, s * 1024:(s + 1) * 1024],
                    probsn[:, s * 1024:(s + 1) * 1024], 1.0,
                    bass.AP(rden, s * H, [[2 * H, 128], [1, 8], [0, 128]]),
                    op0=ALU.mult, op1=ALU.mult)
            P.stage("dve", nrmfn, reads=[Bprobsn[s], Brden], writes=[Bprobsn[s]])

            def ptfn(e, s=s):
                return e.dma_start(
                    probsT[:, s * 1024:(s + 1) * 1024].rearrange(
                        "p (h q) -> p h q", h=H),
                    probsn[:, s * 1024:(s + 1) * 1024].rearrange(
                        "p (h k) -> p h k", h=H),
                    transpose=True)
            P.stage("sp", ptfn, reads=[Bprobsn[s]], writes=[BprobsT[s]])

        def emit_attnout(g):
            for s in range(NSUB):
                for half in range(2):
                    bank, Bbank = next_bank(1)
                    def aofn(e, s=s, half=half, bank=bank):
                        last = None
                        for hh in range(4):
                            h = half * 4 + hh
                            last = e.matmul(
                                bank[:, hh * 128:(hh + 1) * 128],
                                v_tm[:, s * 1024 + h * 128: s * 1024 + (h + 1) * 128],
                                probsT[:, s * 1024 + h * 128: s * 1024 + (h + 1) * 128],
                                start=True, stop=True)
                        return last
                    P.stage("pe", aofn, reads=[Bvtm[s], BprobsT[s]], writes=[Bbank])

                    def aoev(e, s=s, half=half, bank=bank):
                        return e.copy(
                            ao[:, s * 1024 + half * 512: s * 1024 + (half + 1) * 512],
                            bank[:, 0:512])
                    P.stage("act", aoev, reads=[Bbank], writes=[Bao[s]])

        def emit_oproj_s(g, s):
            slot = g % 2
            if True:
                for hf in range(2):
                    bank, Bbank = next_bank(1)
                    def opfn(e, s=s, hf=hf, bank=bank):
                        for kk in range(NCH):
                            e.matmul(
                                bank[:, 0:512],
                                ao[:, s * 1024 + kk * 128: s * 1024 + (kk + 1) * 128],
                                wo[:, kk * D + hf * 512: kk * D + (hf + 1) * 512],
                                start=(kk == 0), stop=False)
                        return e.matmul(
                            bank[:, 0:512], onesb[0:1, 0:128],
                            boeff[0:1, hf * 512:(hf + 1) * 512],
                            start=False, stop=True)
                    P.stage("pe", opfn, reads=[Bao[s], Bwo, Bones, Bboeff],
                            writes=[Bbank])

                    def r1fn(e, s=s, hf=hf, bank=bank, slot=slot):
                        off = s * 1024 + hf * 512
                        return e.scalar_tensor_tensor(
                            r1[:, off:off + 512], bank[:, 0:512], 1.0,
                            x_tm[:, slot * 2048 + off: slot * 2048 + off + 512],
                            op0=ALU.mult, op1=ALU.add,
                            accum_out=rsum1[:, s * 2 + hf: s * 2 + hf + 1])
                    P.stage("dve", r1fn, reads=[Bbank, Bxtm[slot]],
                            writes=[Br[s], Brs1])

        def emit_oproj(g):
            for s in range(NSUB):
                emit_oproj_s(g, s)

        def emit_ln1(g):
            for s in range(NSUB):
                def sqfn(e, s=s):
                    return e.scalar_tensor_tensor(
                        sqs[:, :], r1[:, s * 1024:(s + 1) * 1024], 1.0,
                        r1[:, s * 1024:(s + 1) * 1024],
                        op0=ALU.mult, op1=ALU.mult,
                        accum_out=ssq1[:, s:s + 1])
                P.stage("dve", sqfn, reads=[Br[s]], writes=[Bsqs, Bsq1])
            P.stage("dve", lambda e: e.tensor_tensor(
                msum1[:, 0:2], bass.AP(rsum1, 0, [[4, 128], [2, 2]]),
                bass.AP(rsum1, 1, [[4, 128], [2, 2]]), ALU.add),
                reads=[Brs1], writes=[Bms1])
            P.stage("dve", lambda e: e.tensor_tensor(
                msq1[:, 0:2], msum1[:, 0:2], msum1[:, 0:2], ALU.mult),
                reads=[Bms1], writes=[Bmq1])
            P.stage("dve", lambda e: e.scalar_tensor_tensor(
                varr1[:, 0:2], msq1[:, 0:2], 1.0 / D, ssq1[:, 0:2],
                op0=ALU.mult, op1=ALU.subtract),
                reads=[Bsq1, Bmq1], writes=[Bvr1])
            P.stage("act", lambda e: e.activation(
                lnv1[:, 0:2], varr1[:, 0:2], AF.Ln,
                bias=epsc[:, 0:1], scale=-1.0 / D),
                reads=[Bvr1, Beps], writes=[Blv1])
            P.stage("act", lambda e: e.activation(
                istd1[:, 0:2], lnv1[:, 0:2], AF.Exp, scale=-0.5),
                reads=[Blv1], writes=[Bis1])
            P.stage("dve", lambda e: e.tensor_scalar(
                msq1[:, 2:4], msum1[:, 0:2], -1.0 / D, None, op0=ALU.mult),
                reads=[Bms1], writes=[Bmq1])
            P.stage("dve", lambda e: e.tensor_tensor(
                ngm1[:, 0:2], istd1[:, 0:2], msq1[:, 2:4], ALU.mult),
                reads=[Bis1, Bmq1], writes=[Bng1])
            for s in range(NSUB):
                P.stage("dve", lambda e, s=s: e.tensor_scalar(
                    ln1[:, s * 1024:(s + 1) * 1024],
                    r1[:, s * 1024:(s + 1) * 1024],
                    istd1[:, s:s + 1], ngm1[:, s:s + 1],
                    op0=ALU.mult, op1=ALU.add),
                    reads=[Br[s], Bis1, Bng1], writes=[Bln1[s]])
                P.stage("act", lambda e, s=s: e.activation(
                    ln1bf[:, s * 1024:(s + 1) * 1024],
                    r1[:, s * 1024:(s + 1) * 1024],
                    AF.Identity, bias=ngm1[:, s:s + 1],
                    scale=istd1[:, s:s + 1]),
                    reads=[Br[s], Bis1, Bng1], writes=[Bln1bf[s], Bmrow]
                    if g == 0 else [Bln1bf[s]])

                def lfmfn(e, s=s):
                    return e.dma_start(
                        bass.AP(ln1fm, s * 128, [[2048, 128], [256, 8], [1, 128]]),
                        ln1bf[:, s * 1024:(s + 1) * 1024].rearrange(
                            "p (c f) -> p c f", c=8),
                        transpose=True)
                P.stage("sp", lfmfn, reads=[Bln1bf[s]], writes=[Bln1fm[s]])

        def emit_ffn_s(g, s):
            if True:
                for hf in range(2):
                    bank, Bbank = next_bank(1)
                    def fffn(e, s=s, hf=hf, bank=bank):
                        last = None
                        for kk in range(NCH):
                            last = e.matmul(
                                bank[:, 0:512],
                                ln1fm[:, kk * GTOK + s * 128: kk * GTOK + (s + 1) * 128],
                                swt[:, kk * D + hf * 512: kk * D + (hf + 1) * 512],
                                start=(kk == 0), stop=(kk == NCH - 1))
                        return last
                    P.stage("pe", fffn, reads=[Bln1fm[s]] + Bswt, writes=[Bbank])

                    def r2fn(e, s=s, hf=hf, bank=bank):
                        off = s * 1024 + hf * 512
                        return e.scalar_tensor_tensor(
                            r1[:, off:off + 512], bank[:, 0:512], 1.0,
                            ln1[:, off:off + 512],
                            op0=ALU.mult, op1=ALU.add,
                            accum_out=rsum2[:, s * 2 + hf: s * 2 + hf + 1])
                    P.stage("dve", r2fn, reads=[Bbank, Bln1[s]],
                            writes=[Br[s], Brs2])

        def emit_ffn(g):
            for s in range(NSUB):
                emit_ffn_s(g, s)

        def emit_ln1_s(g, s):
            def sqfn(e, s=s):
                return e.scalar_tensor_tensor(
                    sqs[:, :], r1[:, s * 1024:(s + 1) * 1024], 1.0,
                    r1[:, s * 1024:(s + 1) * 1024],
                    op0=ALU.mult, op1=ALU.mult,
                    accum_out=ssq1[:, s:s + 1])
            P.stage("dve", sqfn, reads=[Br[s]], writes=[Bsqs, Bsq1])
            P.stage("dve", lambda e, s=s: e.tensor_tensor(
                msum1[:, s:s + 1], rsum1[:, 2 * s:2 * s + 1],
                rsum1[:, 2 * s + 1:2 * s + 2], ALU.add),
                reads=[Brs1], writes=[Bms1])
            P.stage("dve", lambda e, s=s: e.tensor_tensor(
                msq1[:, s:s + 1], msum1[:, s:s + 1], msum1[:, s:s + 1],
                ALU.mult),
                reads=[Bms1], writes=[Bmq1])
            P.stage("dve", lambda e, s=s: e.scalar_tensor_tensor(
                varr1[:, s:s + 1], msq1[:, s:s + 1], 1.0 / D, ssq1[:, s:s + 1],
                op0=ALU.mult, op1=ALU.subtract),
                reads=[Bsq1, Bmq1], writes=[Bvr1])
            P.stage("act", lambda e, s=s: e.activation(
                lnv1[:, s:s + 1], varr1[:, s:s + 1], AF.Ln,
                bias=epsc[:, 0:1], scale=-1.0 / D),
                reads=[Bvr1, Beps], writes=[Blv1])
            P.stage("act", lambda e, s=s: e.activation(
                istd1[:, s:s + 1], lnv1[:, s:s + 1], AF.Exp, scale=-0.5),
                reads=[Blv1], writes=[Bis1])
            P.stage("dve", lambda e, s=s: e.tensor_scalar(
                msq1[:, 2 + s:3 + s], msum1[:, s:s + 1], -1.0 / D, None,
                op0=ALU.mult),
                reads=[Bms1], writes=[Bmq1])
            P.stage("dve", lambda e, s=s: e.tensor_tensor(
                ngm1[:, s:s + 1], istd1[:, s:s + 1], msq1[:, 2 + s:3 + s],
                ALU.mult),
                reads=[Bis1, Bmq1], writes=[Bng1])
            P.stage("dve", lambda e, s=s: e.tensor_scalar(
                ln1[:, s * 1024:(s + 1) * 1024],
                r1[:, s * 1024:(s + 1) * 1024],
                istd1[:, s:s + 1], ngm1[:, s:s + 1],
                op0=ALU.mult, op1=ALU.add),
                reads=[Br[s], Bis1, Bng1], writes=[Bln1[s]])
            P.stage("act", lambda e, s=s: e.activation(
                ln1bf[:, s * 1024:(s + 1) * 1024],
                r1[:, s * 1024:(s + 1) * 1024],
                AF.Identity, bias=ngm1[:, s:s + 1],
                scale=istd1[:, s:s + 1]),
                reads=[Br[s], Bis1, Bng1], writes=[Bln1bf[s]])

            def lfmfn(e, s=s):
                return e.dma_start(
                    bass.AP(ln1fm, s * 128, [[2048, 128], [256, 8], [1, 128]]),
                    ln1bf[:, s * 1024:(s + 1) * 1024].rearrange(
                        "p (c f) -> p c f", c=8),
                    transpose=True)
            P.stage("sp", lfmfn, reads=[Bln1bf[s]], writes=[Bln1fm[s]])

        def emit_ln2(g):
            slot = g % 2
            for s in range(NSUB):
                def sq2fn(e, s=s):
                    return e.scalar_tensor_tensor(
                        sqs[:, :], r1[:, s * 1024:(s + 1) * 1024], 1.0,
                        r1[:, s * 1024:(s + 1) * 1024],
                        op0=ALU.mult, op1=ALU.mult,
                        accum_out=ssq2[:, s:s + 1])
                P.stage("dve", sq2fn, reads=[Br[s]], writes=[Bsqs, Bsq2])
            P.stage("dve", lambda e: e.tensor_tensor(
                msum2[:, 0:2], bass.AP(rsum2, 0, [[4, 128], [2, 2]]),
                bass.AP(rsum2, 1, [[4, 128], [2, 2]]), ALU.add),
                reads=[Brs2], writes=[Bms2])
            P.stage("dve", lambda e: e.tensor_tensor(
                msq2[:, 0:2], msum2[:, 0:2], msum2[:, 0:2], ALU.mult),
                reads=[Bms2], writes=[Bmq2])
            P.stage("dve", lambda e: e.scalar_tensor_tensor(
                varr2[:, 0:2], msq2[:, 0:2], 1.0 / D, ssq2[:, 0:2],
                op0=ALU.mult, op1=ALU.subtract),
                reads=[Bsq2, Bmq2], writes=[Bvr2])
            P.stage("act", lambda e: e.activation(
                lnv2[:, 0:2], varr2[:, 0:2], AF.Ln,
                bias=epsc[:, 0:1], scale=-1.0 / D),
                reads=[Bvr2, Beps], writes=[Blv2])
            P.stage("act", lambda e: e.activation(
                istd2[:, 0:2], lnv2[:, 0:2], AF.Exp, scale=-0.5),
                reads=[Blv2], writes=[Bis2])
            P.stage("dve", lambda e: e.tensor_scalar(
                msq2[:, 2:4], msum2[:, 0:2], -1.0 / D, None, op0=ALU.mult),
                reads=[Bms2], writes=[Bmq2])
            P.stage("dve", lambda e: e.tensor_tensor(
                ngm2[:, 0:2], istd2[:, 0:2], msq2[:, 2:4], ALU.mult),
                reads=[Bis2, Bmq2], writes=[Bng2])
            for s in range(NSUB):
                P.stage("dve", lambda e, s=s, slot=slot: e.tensor_scalar(
                    outb[:, slot * 2048 + s * 1024: slot * 2048 + (s + 1) * 1024],
                    r1[:, s * 1024:(s + 1) * 1024],
                    istd2[:, s:s + 1], ngm2[:, s:s + 1],
                    op0=ALU.mult, op1=ALU.add),
                    reads=[Br[s], Bis2, Bng2], writes=[Boutb[slot]])

        def emit_store(g):
            slot = g % 2
            def stfn2(e, g=g, slot=slot):
                return e.dma_start(
                    y_h[g * GTOK:(g + 1) * GTOK, :].rearrange(
                        "(s p) d -> p s d", p=128),
                    outb[:, slot * 2048:(slot + 1) * 2048].rearrange(
                        "p (s d) -> p s d", s=2))
            P.stage("gp", stfn2, reads=[Boutb[slot]])

        emit_xfm(0)
        emit_qk(0, 0)
        emit_qk(0, 1)
        emit_scores_s(0, 0)
        emit_exp_s(0, 0)
        emit_softmax_s(0, 0)
        emit_scores_s(0, 1)
        emit_exp_s(0, 1)
        emit_softmax_s(0, 1)
        for g in range(NG):
            emit_loads(g + 1)
            emit_xfm(g + 1)
            emit_v(g)
            if g == 0:
                emit_boeff()
            emit_attnout(g)
            if g > 0:
                emit_ln2(g - 1)
                emit_store(g - 1)
            if g == NG - 1:
                emit_oproj_s(g, 0)
                emit_ln1_s(g, 0)
                emit_oproj_s(g, 1)
                emit_ffn_s(g, 0)
                emit_ln1_s(g, 1)
                emit_ffn_s(g, 1)
                continue
            emit_oproj(g)
            if g == 0:
                emit_sws_sign(0, 4)
            emit_qk(g + 1, 0)
            if g == 0:
                emit_sws_sign(4, NCH)
            emit_ln1(g)
            emit_qk(g + 1, 1)
            if g == 0:
                emit_sws_trans()
            emit_scores_s(g + 1, 0)
            emit_exp_s(g + 1, 0)
            emit_softmax_s(g + 1, 0)
            emit_ffn(g)
            emit_scores_s(g + 1, 1)
            emit_exp_s(g + 1, 1)
            emit_softmax_s(g + 1, 1)
        slot = (NG - 1) % 2
        for s in range(NSUB):
            def sqln(e, s=s):
                return e.scalar_tensor_tensor(
                    sqs[:, :], r1[:, s * 1024:(s + 1) * 1024], 1.0,
                    r1[:, s * 1024:(s + 1) * 1024],
                    op0=ALU.mult, op1=ALU.mult,
                    accum_out=ssq2[:, s:s + 1])
            P.stage("dve", sqln, reads=[Br[s]], writes=[Bsqs, Bsq2])
            P.stage("dve", lambda e, s=s: e.tensor_tensor(
                msum2[:, s:s + 1], rsum2[:, 2 * s:2 * s + 1],
                rsum2[:, 2 * s + 1:2 * s + 2], ALU.add),
                reads=[Brs2], writes=[Bms2])
            P.stage("dve", lambda e, s=s: e.tensor_tensor(
                msq2[:, s:s + 1], msum2[:, s:s + 1], msum2[:, s:s + 1],
                ALU.mult),
                reads=[Bms2], writes=[Bmq2])
            P.stage("dve", lambda e, s=s: e.scalar_tensor_tensor(
                varr2[:, s:s + 1], msq2[:, s:s + 1], 1.0 / D, ssq2[:, s:s + 1],
                op0=ALU.mult, op1=ALU.subtract),
                reads=[Bsq2, Bmq2], writes=[Bvr2])
            P.stage("act", lambda e, s=s: e.activation(
                lnv2[:, s:s + 1], varr2[:, s:s + 1], AF.Ln,
                bias=epsc[:, 0:1], scale=-1.0 / D),
                reads=[Bvr2, Beps], writes=[Blv2])
            P.stage("act", lambda e, s=s: e.activation(
                istd2[:, s:s + 1], lnv2[:, s:s + 1], AF.Exp, scale=-0.5),
                reads=[Blv2], writes=[Bis2])
            P.stage("dve", lambda e, s=s: e.tensor_scalar(
                msq2[:, 2 + s:3 + s], msum2[:, s:s + 1], -1.0 / D, None,
                op0=ALU.mult),
                reads=[Bms2], writes=[Bmq2])
            P.stage("dve", lambda e, s=s: e.tensor_tensor(
                ngm2[:, s:s + 1], istd2[:, s:s + 1], msq2[:, 2 + s:3 + s],
                ALU.mult),
                reads=[Bis2, Bmq2], writes=[Bng2])
            P.stage("dve", lambda e, s=s, slot=slot: e.tensor_scalar(
                outb[:, slot * 2048 + s * 1024: slot * 2048 + (s + 1) * 1024],
                r1[:, s * 1024:(s + 1) * 1024],
                istd2[:, s:s + 1], ngm2[:, s:s + 1],
                op0=ALU.mult, op1=ALU.add),
                reads=[Br[s], Bis2, Bng2], writes=[Boutb[slot]])

            def stl(e, s=s, slot=slot):
                g = NG - 1
                return e.dma_start(
                    y_h[g * GTOK + s * 128: g * GTOK + (s + 1) * 128, :],
                    outb[:, slot * 2048 + s * 1024: slot * 2048 + (s + 1) * 1024])
            P.stage("gp", stl, reads=[Boutb[slot]])


        P.wait_only("sp")
        P.wait_only("gp")

        nums = sorted(s.num for s in sems.values())
        assert nums == list(range(nums[0], nums[0] + len(nums))), nums
        nc.gpsimd.sem_clear(range(nums[0], nums[-1] + 1))
        nc.all_engine_barrier()

        with nc.Block() as block:
            @block.tensor
            def _(eng):
                P.run("pe", eng, sems)

            @block.scalar
            def _(eng):
                P.run("act", eng, sems)

            @block.vector
            def _(eng):
                P.run("dve", eng, sems)

            @block.sync
            def _(eng):
                P.run("sp", eng, sems)

            @block.gpsimd
            def _(eng):
                P.run("gp", eng, sems)

    return nc


_NC_CACHE = []


def kernel(L0_attn_src, L0_attn_Wq, L0_attn_bq, L0_attn_Wk, L0_attn_bk,
           L0_attn_Wv, L0_attn_bv, L0_attn_Wo, L0_attn_bo, L0_ffn_weight):
    x = np.ascontiguousarray(np.asarray(L0_attn_src))
    ws = {
        "wq": np.ascontiguousarray(np.asarray(L0_attn_Wq)),
        "wk": np.ascontiguousarray(np.asarray(L0_attn_Wk)),
        "wv": np.ascontiguousarray(np.asarray(L0_attn_Wv)),
        "wo": np.ascontiguousarray(np.asarray(L0_attn_Wo)),
        "wf": np.ascontiguousarray(np.asarray(L0_ffn_weight)),
        "bq": np.ascontiguousarray(np.asarray(L0_attn_bq)),
        "bk": np.ascontiguousarray(np.asarray(L0_attn_bk)),
        "bv": np.ascontiguousarray(np.asarray(L0_attn_bv)),
        "bo": np.ascontiguousarray(np.asarray(L0_attn_bo)),
    }
    if not _NC_CACHE:
        _NC_CACHE.append(build_program())
    nc = _NC_CACHE[0]
    in_maps = [dict(ws, x=x[c]) for c in range(8)]
    res = run_bass_kernel_spmd(nc, in_maps, core_ids=list(range(8)))
    return np.stack([np.asarray(res.results[c]["y"]) for c in range(8)], axis=0)


# version 46
# speedup vs baseline: 1.0979x; 1.0367x over previous
import numpy as np
from contextlib import ExitStack

import concourse.bass as bass
import concourse.mybir as mybir
from concourse.bass_utils import run_bass_kernel_spmd

F32 = mybir.dt.float32
BF16 = mybir.dt.bfloat16
AF = mybir.ActivationFunctionType
ALU = mybir.AluOpType
AX = mybir.AxisListType

S = 2048
D = 1024
NCH = 8
H = 8
GTOK = 256
NG = S // GTOK
NSUB = 2
SCALE = float(D // H) ** -0.5
EPS = 1e-5
ENGINES = ("pe", "act", "dve", "sp", "gp", "ra",
           "sp2", "sp3", "gp2", "gp3", "ra2", "ra3")
RING_POOL = {"sp": ("sp", "sp2", "sp3"), "gp": ("gp", "gp2", "gp3"),
             "ra": ("ra", "ra2", "ra3")}
PARANOID = False
DEBUG_DUMP = False


class Buf:
    __slots__ = ("w", "readers", "waw")

    def __init__(self, waw=False):
        self.w = None
        self.readers = {}
        self.waw = waw


class Plan:
    def __init__(self):
        self.ops = {e: [] for e in ENGINES}
        self.cnt = {e: 0 for e in ENGINES}
        self.floor = {e: {} for e in ENGINES}
        self.flushed = {e: 0 for e in ENGINES}
        self.ringn = {}

    def stage(self, eng, fn, reads=(), writes=(), inc=None, sem=None):
        if sem is None:
            sem = eng
        if sem in RING_POOL:
            pool = RING_POOL[sem]
            self.ringn.setdefault(sem, 0)
            key = sem
            sem = pool[self.ringn[key] % len(pool)]
            self.ringn[key] += 1
        if inc is None:
            inc = 16 if sem not in ("pe", "act", "dve") else 1
        waits = {}

        def need(tag, selfwait=False):
            if tag is None:
                return
            e2, v = tag
            if e2 == sem:
                if selfwait and eng in ("act", "dve") and v > self.flushed[eng]:
                    waits[eng] = max(waits.get(eng, 0), v)
                    self.flushed[eng] = v
                return
            if self.floor[eng].get(e2, 0) >= v:
                return
            waits[e2] = max(waits.get(e2, 0), v)

        if PARANOID:
            for e2 in ENGINES:
                if e2 != eng:
                    need((e2, self.cnt[e2]))
        for b in reads:
            need(b.w, selfwait=True)
        for b in writes:
            need(b.w, selfwait=b.waw)
            for e2, v in b.readers.items():
                need((e2, v), selfwait=True)
        for e2, v in waits.items():
            if e2 != sem:
                self.floor[eng][e2] = v
        if sem not in ("pe", "act", "dve") and self.cnt[sem] > 0:
            waits2 = dict(waits)
            waits2[sem] = self.cnt[sem]
            waits = waits2
        self.cnt[sem] += inc
        tag = (sem, self.cnt[sem])
        for b in reads:
            b.readers[sem] = self.cnt[sem]
        for b in writes:
            b.w = tag
            b.readers = {}
        self.ops[eng].append((sorted(waits.items()), fn, inc, sem))
        return tag

    def wait_only(self, eng):
        waits = [(e2, self.cnt[e2]) for e2 in ENGINES
                 if e2 != eng and self.cnt[e2] > self.floor[eng].get(e2, 0)]
        for e2, v in waits:
            self.floor[eng][e2] = v
        self.ops[eng].append((waits, None, 0, eng))

    def run(self, eng, handle, sems):
        for waits, fn, inc, sem in self.ops[eng]:
            for e2, v in waits:
                handle.wait_ge(sems[e2], v)
            if fn is not None:
                last = fn(handle)
                if inc:
                    last.then_inc(sems[sem], inc)


def build_program():
    nc = bass.Bass("TRN2", target_bir_lowering=False)

    x_h = nc.dram_tensor("x", [S, D], F32, kind="ExternalInput").ap()
    wq_h = nc.dram_tensor("wq", [D, D], F32, kind="ExternalInput").ap()
    wk_h = nc.dram_tensor("wk", [D, D], F32, kind="ExternalInput").ap()
    wv_h = nc.dram_tensor("wv", [D, D], F32, kind="ExternalInput").ap()
    wo_h = nc.dram_tensor("wo", [D, D], F32, kind="ExternalInput").ap()
    wf_h = nc.dram_tensor("wf", [D, D], F32, kind="ExternalInput").ap()
    bq_h = nc.dram_tensor("bq", [D], F32, kind="ExternalInput").ap()
    bk_h = nc.dram_tensor("bk", [D], F32, kind="ExternalInput").ap()
    bv_h = nc.dram_tensor("bv", [D], F32, kind="ExternalInput").ap()
    bo_h = nc.dram_tensor("bo", [D], F32, kind="ExternalInput").ap()
    y_h = nc.dram_tensor("y", [S, D], F32, kind="ExternalOutput").ap()
    mscr_h = nc.dram_tensor("mask_scratch", [1, 1280], BF16, kind="Internal").ap()
    dbg_h = {}
    if DEBUG_DUMP:
        for nm, dt_ in (("x_tm", F32), ("x_bf", BF16), ("x_fm", BF16),
                        ("q_fm", BF16), ("k_fm", BF16), ("v_tm", BF16),
                        ("expb", F32), ("probsn", BF16), ("probsT", BF16),
                        ("ao", BF16), ("r1", F32), ("ln1", F32),
                        ("ln1bf", BF16), ("ln1fm", BF16),
                        ("ssum", F32), ("rden", F32)):
            w = 4096 if nm in ("x_tm", "x_bf") else (16 if nm in ("ssum", "rden") else 2048)
            dbg_h[nm] = nc.dram_tensor("dbg_" + nm, [128, w], dt_,
                                       kind="ExternalOutput").ap()

    with ExitStack() as ctx:
        sb = lambda name, shape, dt: ctx.enter_context(nc.sbuf_tensor(name, shape, dt))
        ps = lambda name, shape: ctx.enter_context(nc.psum_tensor(name, shape, F32))

        wq = sb("wq_sb", [128, NCH * D], BF16)
        wk = sb("wk_sb", [128, NCH * D], BF16)
        wv = sb("wv_sb", [128, NCH * D], BF16)
        wo = sb("wo_sb", [128, NCH * D], BF16)
        swt = sb("swt_sb", [128, NCH * D], BF16)
        sws = sb("sws_sb", [128, NCH * D], BF16)
        brt = sb("brt", [128, 128], F32)
        bcol = sb("bcol", [128, 128], F32)
        bvc = sb("bv_bf", [128, NCH], BF16)
        borow = sb("bo_row", [1, D], F32)
        boeff = sb("boeff", [1, D], BF16)
        onesb = sb("ones_bf", [1, 256], BF16)
        masku = sb("masku", [2, 128], BF16)
        maskv = sb("maskv", [2, 512], BF16)

        x_tm = sb("x_tm", [128, 4096], F32)
        x_bf = sb("x_bf", [128, 4096], BF16)
        x_fm = sb("x_fm", [128, 4096], BF16)
        q_fm = sb("q_fm", [128, 2048], BF16)
        k_fm = sb("k_fm", [128, 2048], BF16)
        v_tm = sb("v_tm", [128, 2048], BF16)
        probsn = sb("probsn", [128, 2048], BF16)
        probsT = sb("probsT", [128, 2048], BF16)
        ao = sb("ao_sb", [128, 2048], BF16)
        r1 = sb("r1", [128, 2048], F32)
        ln1 = sb("ln1", [128, 2048], F32)
        ln1bf = sb("ln1bf", [128, 2048], BF16)
        ln1fm = sb("ln1fm", [128, 2048], BF16)
        outb = sb("outb", [128, 4096], F32)
        sqs = sb("sqs", [128, 1024], BF16)
        epsc = sb("epsc", [128, 1], F32)

        ssum = sb("ssum", [128, 2 * H], F32)
        rden = sb("rden", [128, 2 * H], F32)
        rsum1 = sb("rsum1", [128, 4], F32)
        ssq1 = sb("ssq1", [128, 2], F32)
        msum1 = sb("msum1", [128, 2], F32)
        msq1 = sb("msq1", [128, 4], F32)
        varr1 = sb("varr1", [128, 2], F32)
        lnv1 = sb("lnv1", [128, 2], F32)
        istd1 = sb("istd1", [128, 2], F32)
        ngm1 = sb("ngm1", [128, 2], F32)
        rsum2 = sb("rsum2", [128, 4], F32)
        ssq2 = sb("ssq2", [128, 2], F32)
        msum2 = sb("msum2", [128, 2], F32)
        msq2 = sb("msq2", [128, 4], F32)
        varr2 = sb("varr2", [128, 2], F32)
        lnv2 = sb("lnv2", [128, 2], F32)
        istd2 = sb("istd2", [128, 2], F32)
        ngm2 = sb("ngm2", [128, 2], F32)

        pb = [ps(f"pb{i}", [128, 512]) for i in range(6)]
        psc = ps("psc", [128, 1024])

        sems = {e: ctx.enter_context(nc.semaphore(f"sem_{e}")) for e in ENGINES}

        P = Plan()
        Bwq, Bwk, Bwv, Bwo = (Buf() for _ in range(4))
        Bsws = [Buf() for _ in range(NCH)]
        Bsg = [Buf() for _ in range(NCH)]
        Bswt = [Buf() for _ in range(NCH)]
        Bbq, Bbk, Bbv, Bborow, Bboeff, Bones = (Buf() for _ in range(6))
        Bbrt, Bbcol, Bbvc = Buf(), Buf(), Buf()
        Beps = Buf()
        Bmask = Buf()
        Bxtm = [Buf(), Buf()]
        Bxbf = [Buf(), Buf()]
        Boutb = [Buf(), Buf()]
        Bqfm, Bkfm = Buf(), Buf()
        Bln1fm = [Buf(), Buf()]
        Bsqs = Buf(waw=True)
        Bxfm = [[Buf(), Buf()], [Buf(), Buf()]]
        Bvtm = [Buf(), Buf()]
        Bprobsn = [Buf(), Buf()]
        BprobsT = [Buf(), Buf()]
        Bao = [Buf(), Buf()]
        Br = [Buf(), Buf()]
        Bln1 = [Buf(), Buf()]
        Bln1bf = [Buf(), Buf()]
        Bssum, Brden = Buf(), Buf()
        Brs1, Bsq1, Bms1, Bmq1, Bvr1, Blv1, Bis1, Bng1 = (Buf() for _ in range(8))
        Brs2, Bsq2, Bms2, Bmq2, Bvr2, Blv2, Bis2, Bng2 = (Buf() for _ in range(8))
        Bpb = [Buf() for _ in range(6)]
        Bpsc = Buf()

        rot = [0, 0]

        def next_bank(pool):
            if pool == 0:
                i = rot[0] % 3
            else:
                i = 3 + rot[1] % 3
            rot[pool] += 1
            return pb[i], Bpb[i]

        def ld_x(g, slot, out_t, cast):
            def fn(e, g=g, slot=slot, out_t=out_t):
                return e.dma_start(
                    out_t[:, slot * 2048:(slot + 1) * 2048].rearrange(
                        "p (s d) -> p s d", s=2),
                    x_h[g * GTOK:(g + 1) * GTOK, :].rearrange(
                        "(s p) d -> p s d", p=128),
                )
            return fn

        def ld_w(wt, wh, Bw):
            def fn(e, wt=wt, wh=wh):
                return e.dma_start(
                    wt[:, :].rearrange("p (k o) -> p k o", k=8),
                    wh.rearrange("(k p) o -> p k o", p=128))
            P.stage("gp", fn, writes=[Bw])

        def ld_b(bh, base, Bb):
            def fn(e, bh=bh, base=base):
                return e.dma_start(
                    brt[base:base + 8, :], bh.rearrange("(p f) -> p f", p=8))
            P.stage("sp", fn, reads=[Bbrt], writes=[Bb])

        P.stage("gp", ld_x(0, 0, x_bf, True), writes=[Bxbf[0]])
        ld_w(wq, wq_h, Bwq)
        P.stage("dve", lambda e: e.memset(brt[:, :], 0.0), writes=[Bbrt])
        ld_b(bq_h, 0, Bbq)
        ld_b(bk_h, 32, Bbk)
        ld_b(bv_h, 64, Bbv)
        for rb, Bb in ((0, Bbq), (32, Bbk), (64, Bbv)):
            for J in range(4):
                P.stage("dve", lambda e, rb=rb, J=J: e.transpose(
                    bcol[32 * J:32 * J + 32, rb:rb + 32],
                    brt[rb:rb + 32, 32 * J:32 * J + 32]),
                    reads=[Bb, Bbrt], writes=[Bbcol])
        P.stage("dve", lambda e: e.tensor_copy(bvc[:, :], bcol[:, 64:64 + NCH]),
                reads=[Bbcol], writes=[Bbvc])
        ld_w(wk, wk_h, Bwk)
        P.stage("gp", lambda e: e.dma_start(
            borow[:, :], bo_h.rearrange("(one d) -> one d", one=1)), writes=[Bborow])
        ld_w(wv, wv_h, Bwv)
        P.stage("gp", ld_x(0, 0, x_tm, False), writes=[Bxtm[0]])
        ld_w(wo, wo_h, Bwo)

        P.stage("dve", lambda e: e.memset(onesb[:, :], 1.0), writes=[Bones])

        Bmrow = Buf()
        mrow = ln1bf

        def mkmrow(e):
            e.memset(mrow[0:1, 0:64], 1.0)
            e.memset(mrow[0:1, 64:192], 0.0)
            e.memset(mrow[0:1, 192:256], 1.0)
            last = None
            for b in range(4):
                base = 256 + b * 128
                e.memset(mrow[0:1, base:base + 64], 0.0)
                e.memset(mrow[0:1, base + 64:base + 128], -30000.0)
                base = 768 + b * 128
                e.memset(mrow[0:1, base:base + 64], -30000.0)
                last = e.memset(mrow[0:1, base + 64:base + 128], 0.0)
            return last
        P.stage("dve", mkmrow, writes=[Bmrow])
        Bmscr = Buf()
        P.stage("sp", lambda e: e.dma_start(mscr_h, mrow[0:1, 0:1280]),
                reads=[Bmrow], writes=[Bmscr])
        P.stage("gp", lambda e: e.dma_start(
            masku[:, :], bass.AP(mscr_h.tensor, 0, [[128, 2], [1, 128]])),
            reads=[Bmscr], writes=[Bmask])
        P.stage("gp", lambda e: e.dma_start(
            maskv[:, :],
            bass.AP(mscr_h.tensor, 256, [[512, 2], [1, 512]])),
            reads=[Bmscr], writes=[Bmask])
        P.stage("dve", lambda e: e.memset(epsc[:, :], EPS), writes=[Beps])
        P.stage("dve", lambda e: e.memset(probsn[:, :], 0.0),
                writes=[Bprobsn[0], Bprobsn[1]])

        for oc in range(NCH):
            P.stage("gp", lambda e, oc=oc: e.dma_start(
                sws[:, oc * D:(oc + 1) * D],
                wf_h[oc * 128:(oc + 1) * 128, :]), writes=[Bsws[oc]])

        def emit_sws_sign(lo, hi):
            for oc in range(lo, hi):
                P.stage("act", lambda e, oc=oc: e.activation(
                    sws[:, oc * D:(oc + 1) * D], sws[:, oc * D:(oc + 1) * D],
                    AF.Sign),
                    reads=[Bsws[oc]], writes=[Bsg[oc]])

        def emit_sws_trans():
            for oc in range(NCH):
                def fn(e, oc=oc):
                    return e.dma_start(
                        bass.AP(swt, oc * 128, [[NCH * D, 128], [D, 8], [1, 128]]),
                        sws[:, oc * D:(oc + 1) * D].rearrange(
                            "p (c i) -> p c i", c=8),
                        transpose=True)
                P.stage("sp", fn, reads=[Bsg[oc]], writes=[Bswt[oc]])

        def emit_loads(g):
            if g >= NG:
                return
            ns = g % 2
            P.stage("gp", ld_x(g, ns, x_bf, True), writes=[Bxbf[ns]])
            P.stage("gp", ld_x(g, ns, x_tm, False), writes=[Bxtm[ns]])

        def emit_xfm(g):
            if g >= NG:
                return
            slot = g % 2
            for s in range(NSUB):
                def fn(e, s=s, slot=slot):
                    return e.dma_start(
                        bass.AP(x_fm, slot * 2048 + s * 128,
                                [[4096, 128], [256, 8], [1, 128]]),
                        x_bf[:, slot * 2048 + s * 1024: slot * 2048 + (s + 1) * 1024
                             ].rearrange("p (c f) -> p c f", c=8),
                        transpose=True)
                P.stage("sp", fn, reads=[Bxbf[slot]], writes=[Bxfm[slot][s]])

        def emit_qk(g, which):
            if g >= NG:
                return
            sel = ((wq, Bwq, 0, q_fm, Bqfm),
                   (wk, Bwk, 32, k_fm, Bkfm))
            for (wt, Bw, boff, dst, Bdst) in sel[which:which + 1]:
                for m2 in range(NCH // 2):
                    bank, Bbank = next_bank(0)
                    slot = g % 2
                    for mi in range(2):
                        m = m2 * 2 + mi
                        def mmfn(e, wt=wt, m=m, mi=mi, bank=bank, slot=slot):
                            last = None
                            for kk in range(NCH):
                                last = e.matmul(
                                    bank[:, mi * GTOK:(mi + 1) * GTOK],
                                    wt[:, kk * D + m * 128: kk * D + (m + 1) * 128],
                                    x_fm[:, slot * 2048 + kk * GTOK:
                                         slot * 2048 + (kk + 1) * GTOK],
                                    start=(kk == 0), stop=(kk == NCH - 1))
                            return last
                        P.stage("pe", mmfn,
                                reads=[Bw, Bxfm[slot][0], Bxfm[slot][1]],
                                writes=[Bbank])
                    for mi in range(2):
                        m = m2 * 2 + mi
                        def evfn(e, m=m, mi=mi, bank=bank, boff=boff, dst=dst):
                            return e.activation(
                                dst[:, m * GTOK:(m + 1) * GTOK],
                                bank[:, mi * GTOK:(mi + 1) * GTOK],
                                AF.Identity,
                                bias=bcol[:, boff + m:boff + m + 1], scale=1.0)
                        P.stage("act", evfn, reads=[Bbank, Bbcol], writes=[Bdst])

        def emit_v(g):
            for s in range(NSUB):
                for hf in range(2):
                    bank, Bbank = next_bank(0)
                    slot = g % 2
                    def mmfn(e, s=s, hf=hf, bank=bank, slot=slot):
                        last = None
                        for kk in range(NCH):
                            last = e.matmul(
                                bank[:, 0:512],
                                x_fm[:, slot * 2048 + kk * GTOK + s * 128:
                                     slot * 2048 + kk * GTOK + (s + 1) * 128],
                                wv[:, kk * D + hf * 512: kk * D + (hf + 1) * 512],
                                start=(kk == 0), stop=(kk == NCH - 1))
                        return last
                    P.stage("pe", mmfn, reads=[Bwv, Bxfm[slot][s]], writes=[Bbank])

                    def evfn(e, s=s, hf=hf, bank=bank):
                        return e.copy(
                            v_tm[:, s * 1024 + hf * 512: s * 1024 + (hf + 1) * 512],
                            bank[:, 0:512])
                    P.stage("act", evfn, reads=[Bbank], writes=[Bvtm[s]])

        def emit_boeff():
            for hf in range(2):
                bank, Bbank = next_bank(1)
                def mmfn(e, hf=hf, bank=bank):
                    last = None
                    for kk in range(NCH):
                        last = e.matmul(
                            bank[0:1, 0:512],
                            bvc[:, kk:kk + 1],
                            wo[:, kk * D + hf * 512: kk * D + (hf + 1) * 512],
                            start=(kk == 0), stop=(kk == NCH - 1))
                    return last
                P.stage("pe", mmfn, reads=[Bbvc, Bwo], writes=[Bbank])

                def ttfn(e, hf=hf, bank=bank):
                    return e.tensor_tensor(
                        boeff[0:1, hf * 512:(hf + 1) * 512],
                        bank[0:1, 0:512],
                        borow[0:1, hf * 512:(hf + 1) * 512], ALU.add)
                P.stage("dve", ttfn, reads=[Bbank, Bborow], writes=[Bboeff])

        def emit_scores_s(g, s):
            def scfn(e, s=s):
                last = None
                for half in range(2):
                    for hh in range(4):
                        h = half * 4 + hh
                        e.matmul(
                            psc[:, h * 128:(h + 1) * 128],
                            q_fm[:, h * GTOK + s * 128: h * GTOK + (s + 1) * 128],
                            k_fm[:, h * GTOK + s * 128: h * GTOK + (s + 1) * 128],
                            start=(hh == 0), stop=False)
                    last = e.matmul(
                        psc[:, half * 512:(half + 1) * 512],
                        masku[:, :], maskv[:, :],
                        start=False, stop=True)
                return last
            P.stage("pe", scfn, reads=[Bqfm, Bkfm, Bmask], writes=[Bpsc])

        def emit_exp_s(g, s):
            def expfn(e, s=s):
                return e.activation(
                    probsn[:, s * 1024:(s + 1) * 1024], psc[:, :],
                    AF.Exp, scale=SCALE)
            P.stage("act", expfn, reads=[Bpsc], writes=[Bprobsn[s]])

        def emit_softmax_s(g, s):
            def sumfn(e, s=s):
                return e.reduce_sum(
                    ssum[:, s * H:(s + 1) * H],
                    probsn[:, s * 1024:(s + 1) * 1024].rearrange(
                        "p (h k) -> p h k", h=H),
                    axis=AX.X)
            P.stage("dve", sumfn, reads=[Bprobsn[s]], writes=[Bssum])
            P.stage("dve", lambda e, s=s: e.reciprocal(
                rden[:, s * H:(s + 1) * H], ssum[:, s * H:(s + 1) * H]),
                reads=[Bssum], writes=[Brden])

            def nrmfn(e, s=s):
                return e.scalar_tensor_tensor(
                    probsn[:, s * 1024:(s + 1) * 1024],
                    probsn[:, s * 1024:(s + 1) * 1024], 1.0,
                    bass.AP(rden, s * H, [[2 * H, 128], [1, 8], [0, 128]]),
                    op0=ALU.mult, op1=ALU.mult)
            P.stage("dve", nrmfn, reads=[Bprobsn[s], Brden], writes=[Bprobsn[s]])

            def ptfn(e, s=s):
                return e.dma_start(
                    probsT[:, s * 1024:(s + 1) * 1024].rearrange(
                        "p (h q) -> p h q", h=H),
                    probsn[:, s * 1024:(s + 1) * 1024].rearrange(
                        "p (h k) -> p h k", h=H),
                    transpose=True)
            P.stage("sp", ptfn, reads=[Bprobsn[s]], writes=[BprobsT[s]])

        def emit_attnout(g):
            for s in range(NSUB):
                for half in range(2):
                    bank, Bbank = next_bank(1)
                    def aofn(e, s=s, half=half, bank=bank):
                        last = None
                        for hh in range(4):
                            h = half * 4 + hh
                            last = e.matmul(
                                bank[:, hh * 128:(hh + 1) * 128],
                                v_tm[:, s * 1024 + h * 128: s * 1024 + (h + 1) * 128],
                                probsT[:, s * 1024 + h * 128: s * 1024 + (h + 1) * 128],
                                start=True, stop=True)
                        return last
                    P.stage("pe", aofn, reads=[Bvtm[s], BprobsT[s]], writes=[Bbank])

                    def aoev(e, s=s, half=half, bank=bank):
                        return e.copy(
                            ao[:, s * 1024 + half * 512: s * 1024 + (half + 1) * 512],
                            bank[:, 0:512])
                    P.stage("act", aoev, reads=[Bbank], writes=[Bao[s]])

        def emit_oproj_s(g, s):
            slot = g % 2
            if True:
                for hf in range(2):
                    bank, Bbank = next_bank(1)
                    def opfn(e, s=s, hf=hf, bank=bank):
                        for kk in range(NCH):
                            e.matmul(
                                bank[:, 0:512],
                                ao[:, s * 1024 + kk * 128: s * 1024 + (kk + 1) * 128],
                                wo[:, kk * D + hf * 512: kk * D + (hf + 1) * 512],
                                start=(kk == 0), stop=False)
                        return e.matmul(
                            bank[:, 0:512], onesb[0:1, 0:128],
                            boeff[0:1, hf * 512:(hf + 1) * 512],
                            start=False, stop=True)
                    P.stage("pe", opfn, reads=[Bao[s], Bwo, Bones, Bboeff],
                            writes=[Bbank])

                    def r1fn(e, s=s, hf=hf, bank=bank, slot=slot):
                        off = s * 1024 + hf * 512
                        return e.scalar_tensor_tensor(
                            r1[:, off:off + 512], bank[:, 0:512], 1.0,
                            x_tm[:, slot * 2048 + off: slot * 2048 + off + 512],
                            op0=ALU.mult, op1=ALU.add,
                            accum_out=rsum1[:, s * 2 + hf: s * 2 + hf + 1])
                    P.stage("dve", r1fn, reads=[Bbank, Bxtm[slot]],
                            writes=[Br[s], Brs1])

        def emit_oproj(g):
            for s in range(NSUB):
                emit_oproj_s(g, s)

        def emit_ln1(g):
            for s in range(NSUB):
                def sqfn(e, s=s):
                    return e.scalar_tensor_tensor(
                        sqs[:, :], r1[:, s * 1024:(s + 1) * 1024], 1.0,
                        r1[:, s * 1024:(s + 1) * 1024],
                        op0=ALU.mult, op1=ALU.mult,
                        accum_out=ssq1[:, s:s + 1])
                P.stage("dve", sqfn, reads=[Br[s]], writes=[Bsqs, Bsq1])
            P.stage("dve", lambda e: e.tensor_tensor(
                msum1[:, 0:2], bass.AP(rsum1, 0, [[4, 128], [2, 2]]),
                bass.AP(rsum1, 1, [[4, 128], [2, 2]]), ALU.add),
                reads=[Brs1], writes=[Bms1])
            P.stage("dve", lambda e: e.tensor_tensor(
                msq1[:, 0:2], msum1[:, 0:2], msum1[:, 0:2], ALU.mult),
                reads=[Bms1], writes=[Bmq1])
            P.stage("dve", lambda e: e.scalar_tensor_tensor(
                varr1[:, 0:2], msq1[:, 0:2], 1.0 / D, ssq1[:, 0:2],
                op0=ALU.mult, op1=ALU.subtract),
                reads=[Bsq1, Bmq1], writes=[Bvr1])
            P.stage("act", lambda e: e.activation(
                lnv1[:, 0:2], varr1[:, 0:2], AF.Ln,
                bias=epsc[:, 0:1], scale=-1.0 / D),
                reads=[Bvr1, Beps], writes=[Blv1])
            P.stage("act", lambda e: e.activation(
                istd1[:, 0:2], lnv1[:, 0:2], AF.Exp, scale=-0.5),
                reads=[Blv1], writes=[Bis1])
            P.stage("dve", lambda e: e.tensor_scalar(
                msq1[:, 2:4], msum1[:, 0:2], -1.0 / D, None, op0=ALU.mult),
                reads=[Bms1], writes=[Bmq1])
            P.stage("dve", lambda e: e.tensor_tensor(
                ngm1[:, 0:2], istd1[:, 0:2], msq1[:, 2:4], ALU.mult),
                reads=[Bis1, Bmq1], writes=[Bng1])
            for s in range(NSUB):
                P.stage("dve", lambda e, s=s: e.tensor_scalar(
                    ln1[:, s * 1024:(s + 1) * 1024],
                    r1[:, s * 1024:(s + 1) * 1024],
                    istd1[:, s:s + 1], ngm1[:, s:s + 1],
                    op0=ALU.mult, op1=ALU.add),
                    reads=[Br[s], Bis1, Bng1], writes=[Bln1[s]])
                P.stage("act", lambda e, s=s: e.activation(
                    ln1bf[:, s * 1024:(s + 1) * 1024],
                    r1[:, s * 1024:(s + 1) * 1024],
                    AF.Identity, bias=ngm1[:, s:s + 1],
                    scale=istd1[:, s:s + 1]),
                    reads=[Br[s], Bis1, Bng1], writes=[Bln1bf[s], Bmrow]
                    if g == 0 else [Bln1bf[s]])

                def lfmfn(e, s=s):
                    return e.dma_start(
                        bass.AP(ln1fm, s * 128, [[2048, 128], [256, 8], [1, 128]]),
                        ln1bf[:, s * 1024:(s + 1) * 1024].rearrange(
                            "p (c f) -> p c f", c=8),
                        transpose=True)
                P.stage("sp", lfmfn, reads=[Bln1bf[s]], writes=[Bln1fm[s]])

        def emit_ffn_s(g, s):
            if True:
                for hf in range(2):
                    bank, Bbank = next_bank(1)
                    def fffn(e, s=s, hf=hf, bank=bank):
                        last = None
                        for kk in range(NCH):
                            last = e.matmul(
                                bank[:, 0:512],
                                ln1fm[:, kk * GTOK + s * 128: kk * GTOK + (s + 1) * 128],
                                swt[:, kk * D + hf * 512: kk * D + (hf + 1) * 512],
                                start=(kk == 0), stop=(kk == NCH - 1))
                        return last
                    P.stage("pe", fffn, reads=[Bln1fm[s]] + Bswt, writes=[Bbank])

                    def r2fn(e, s=s, hf=hf, bank=bank):
                        off = s * 1024 + hf * 512
                        return e.scalar_tensor_tensor(
                            r1[:, off:off + 512], bank[:, 0:512], 1.0,
                            ln1[:, off:off + 512],
                            op0=ALU.mult, op1=ALU.add,
                            accum_out=rsum2[:, s * 2 + hf: s * 2 + hf + 1])
                    P.stage("dve", r2fn, reads=[Bbank, Bln1[s]],
                            writes=[Br[s], Brs2])

        def emit_ffn(g):
            for s in range(NSUB):
                emit_ffn_s(g, s)

        def emit_ln1_s(g, s):
            def sqfn(e, s=s):
                return e.scalar_tensor_tensor(
                    sqs[:, :], r1[:, s * 1024:(s + 1) * 1024], 1.0,
                    r1[:, s * 1024:(s + 1) * 1024],
                    op0=ALU.mult, op1=ALU.mult,
                    accum_out=ssq1[:, s:s + 1])
            P.stage("dve", sqfn, reads=[Br[s]], writes=[Bsqs, Bsq1])
            P.stage("dve", lambda e, s=s: e.tensor_tensor(
                msum1[:, s:s + 1], rsum1[:, 2 * s:2 * s + 1],
                rsum1[:, 2 * s + 1:2 * s + 2], ALU.add),
                reads=[Brs1], writes=[Bms1])
            P.stage("dve", lambda e, s=s: e.tensor_tensor(
                msq1[:, s:s + 1], msum1[:, s:s + 1], msum1[:, s:s + 1],
                ALU.mult),
                reads=[Bms1], writes=[Bmq1])
            P.stage("dve", lambda e, s=s: e.scalar_tensor_tensor(
                varr1[:, s:s + 1], msq1[:, s:s + 1], 1.0 / D, ssq1[:, s:s + 1],
                op0=ALU.mult, op1=ALU.subtract),
                reads=[Bsq1, Bmq1], writes=[Bvr1])
            P.stage("act", lambda e, s=s: e.activation(
                lnv1[:, s:s + 1], varr1[:, s:s + 1], AF.Ln,
                bias=epsc[:, 0:1], scale=-1.0 / D),
                reads=[Bvr1, Beps], writes=[Blv1])
            P.stage("act", lambda e, s=s: e.activation(
                istd1[:, s:s + 1], lnv1[:, s:s + 1], AF.Exp, scale=-0.5),
                reads=[Blv1], writes=[Bis1])
            P.stage("dve", lambda e, s=s: e.tensor_scalar(
                msq1[:, 2 + s:3 + s], msum1[:, s:s + 1], -1.0 / D, None,
                op0=ALU.mult),
                reads=[Bms1], writes=[Bmq1])
            P.stage("dve", lambda e, s=s: e.tensor_tensor(
                ngm1[:, s:s + 1], istd1[:, s:s + 1], msq1[:, 2 + s:3 + s],
                ALU.mult),
                reads=[Bis1, Bmq1], writes=[Bng1])
            P.stage("dve", lambda e, s=s: e.tensor_scalar(
                ln1[:, s * 1024:(s + 1) * 1024],
                r1[:, s * 1024:(s + 1) * 1024],
                istd1[:, s:s + 1], ngm1[:, s:s + 1],
                op0=ALU.mult, op1=ALU.add),
                reads=[Br[s], Bis1, Bng1], writes=[Bln1[s]])
            P.stage("act", lambda e, s=s: e.activation(
                ln1bf[:, s * 1024:(s + 1) * 1024],
                r1[:, s * 1024:(s + 1) * 1024],
                AF.Identity, bias=ngm1[:, s:s + 1],
                scale=istd1[:, s:s + 1]),
                reads=[Br[s], Bis1, Bng1], writes=[Bln1bf[s]])

            def lfmfn(e, s=s):
                return e.dma_start(
                    bass.AP(ln1fm, s * 128, [[2048, 128], [256, 8], [1, 128]]),
                    ln1bf[:, s * 1024:(s + 1) * 1024].rearrange(
                        "p (c f) -> p c f", c=8),
                    transpose=True)
            P.stage("sp", lfmfn, reads=[Bln1bf[s]], writes=[Bln1fm[s]])

        def emit_ln2(g):
            slot = g % 2
            for s in range(NSUB):
                def sq2fn(e, s=s):
                    return e.scalar_tensor_tensor(
                        sqs[:, :], r1[:, s * 1024:(s + 1) * 1024], 1.0,
                        r1[:, s * 1024:(s + 1) * 1024],
                        op0=ALU.mult, op1=ALU.mult,
                        accum_out=ssq2[:, s:s + 1])
                P.stage("dve", sq2fn, reads=[Br[s]], writes=[Bsqs, Bsq2])
            P.stage("dve", lambda e: e.tensor_tensor(
                msum2[:, 0:2], bass.AP(rsum2, 0, [[4, 128], [2, 2]]),
                bass.AP(rsum2, 1, [[4, 128], [2, 2]]), ALU.add),
                reads=[Brs2], writes=[Bms2])
            P.stage("dve", lambda e: e.tensor_tensor(
                msq2[:, 0:2], msum2[:, 0:2], msum2[:, 0:2], ALU.mult),
                reads=[Bms2], writes=[Bmq2])
            P.stage("dve", lambda e: e.scalar_tensor_tensor(
                varr2[:, 0:2], msq2[:, 0:2], 1.0 / D, ssq2[:, 0:2],
                op0=ALU.mult, op1=ALU.subtract),
                reads=[Bsq2, Bmq2], writes=[Bvr2])
            P.stage("act", lambda e: e.activation(
                lnv2[:, 0:2], varr2[:, 0:2], AF.Ln,
                bias=epsc[:, 0:1], scale=-1.0 / D),
                reads=[Bvr2, Beps], writes=[Blv2])
            P.stage("act", lambda e: e.activation(
                istd2[:, 0:2], lnv2[:, 0:2], AF.Exp, scale=-0.5),
                reads=[Blv2], writes=[Bis2])
            P.stage("dve", lambda e: e.tensor_scalar(
                msq2[:, 2:4], msum2[:, 0:2], -1.0 / D, None, op0=ALU.mult),
                reads=[Bms2], writes=[Bmq2])
            P.stage("dve", lambda e: e.tensor_tensor(
                ngm2[:, 0:2], istd2[:, 0:2], msq2[:, 2:4], ALU.mult),
                reads=[Bis2, Bmq2], writes=[Bng2])
            for s in range(NSUB):
                P.stage("dve", lambda e, s=s, slot=slot: e.tensor_scalar(
                    outb[:, slot * 2048 + s * 1024: slot * 2048 + (s + 1) * 1024],
                    r1[:, s * 1024:(s + 1) * 1024],
                    istd2[:, s:s + 1], ngm2[:, s:s + 1],
                    op0=ALU.mult, op1=ALU.add),
                    reads=[Br[s], Bis2, Bng2], writes=[Boutb[slot]])

        def emit_store(g):
            slot = g % 2
            def stfn2(e, g=g, slot=slot):
                return e.dma_start(
                    y_h[g * GTOK:(g + 1) * GTOK, :].rearrange(
                        "(s p) d -> p s d", p=128),
                    outb[:, slot * 2048:(slot + 1) * 2048].rearrange(
                        "p (s d) -> p s d", s=2))
            P.stage("gp", stfn2, reads=[Boutb[slot]])

        emit_xfm(0)
        emit_qk(0, 0)
        emit_qk(0, 1)
        emit_scores_s(0, 0)
        emit_exp_s(0, 0)
        emit_softmax_s(0, 0)
        emit_scores_s(0, 1)
        emit_exp_s(0, 1)
        emit_softmax_s(0, 1)
        for g in range(NG):
            emit_loads(g + 1)
            emit_xfm(g + 1)
            emit_v(g)
            if g == 0:
                emit_boeff()
            emit_attnout(g)
            if g > 0:
                emit_ln2(g - 1)
                emit_store(g - 1)
            if g == NG - 1:
                emit_oproj_s(g, 0)
                emit_ln1_s(g, 0)
                emit_oproj_s(g, 1)
                emit_ffn_s(g, 0)
                emit_ln1_s(g, 1)
                emit_ffn_s(g, 1)
                continue
            emit_oproj(g)
            if g == 0:
                emit_sws_sign(0, 4)
            emit_qk(g + 1, 0)
            if g == 0:
                emit_sws_sign(4, NCH)
            emit_ln1(g)
            emit_qk(g + 1, 1)
            if g == 0:
                emit_sws_trans()
            emit_ffn_s(g, 0)
            emit_scores_s(g + 1, 0)
            emit_exp_s(g + 1, 0)
            emit_softmax_s(g + 1, 0)
            emit_ffn_s(g, 1)
            emit_scores_s(g + 1, 1)
            emit_exp_s(g + 1, 1)
            emit_softmax_s(g + 1, 1)
        slot = (NG - 1) % 2
        for s in range(NSUB):
            def sqln(e, s=s):
                return e.scalar_tensor_tensor(
                    sqs[:, :], r1[:, s * 1024:(s + 1) * 1024], 1.0,
                    r1[:, s * 1024:(s + 1) * 1024],
                    op0=ALU.mult, op1=ALU.mult,
                    accum_out=ssq2[:, s:s + 1])
            P.stage("dve", sqln, reads=[Br[s]], writes=[Bsqs, Bsq2])
            P.stage("dve", lambda e, s=s: e.tensor_tensor(
                msum2[:, s:s + 1], rsum2[:, 2 * s:2 * s + 1],
                rsum2[:, 2 * s + 1:2 * s + 2], ALU.add),
                reads=[Brs2], writes=[Bms2])
            P.stage("dve", lambda e, s=s: e.tensor_tensor(
                msq2[:, s:s + 1], msum2[:, s:s + 1], msum2[:, s:s + 1],
                ALU.mult),
                reads=[Bms2], writes=[Bmq2])
            P.stage("dve", lambda e, s=s: e.scalar_tensor_tensor(
                varr2[:, s:s + 1], msq2[:, s:s + 1], 1.0 / D, ssq2[:, s:s + 1],
                op0=ALU.mult, op1=ALU.subtract),
                reads=[Bsq2, Bmq2], writes=[Bvr2])
            P.stage("act", lambda e, s=s: e.activation(
                lnv2[:, s:s + 1], varr2[:, s:s + 1], AF.Ln,
                bias=epsc[:, 0:1], scale=-1.0 / D),
                reads=[Bvr2, Beps], writes=[Blv2])
            P.stage("act", lambda e, s=s: e.activation(
                istd2[:, s:s + 1], lnv2[:, s:s + 1], AF.Exp, scale=-0.5),
                reads=[Blv2], writes=[Bis2])
            P.stage("dve", lambda e, s=s: e.tensor_scalar(
                msq2[:, 2 + s:3 + s], msum2[:, s:s + 1], -1.0 / D, None,
                op0=ALU.mult),
                reads=[Bms2], writes=[Bmq2])
            P.stage("dve", lambda e, s=s: e.tensor_tensor(
                ngm2[:, s:s + 1], istd2[:, s:s + 1], msq2[:, 2 + s:3 + s],
                ALU.mult),
                reads=[Bis2, Bmq2], writes=[Bng2])
            P.stage("dve", lambda e, s=s, slot=slot: e.tensor_scalar(
                outb[:, slot * 2048 + s * 1024: slot * 2048 + (s + 1) * 1024],
                r1[:, s * 1024:(s + 1) * 1024],
                istd2[:, s:s + 1], ngm2[:, s:s + 1],
                op0=ALU.mult, op1=ALU.add),
                reads=[Br[s], Bis2, Bng2], writes=[Boutb[slot]])

            def stl(e, s=s, slot=slot):
                g = NG - 1
                return e.dma_start(
                    y_h[g * GTOK + s * 128: g * GTOK + (s + 1) * 128, :],
                    outb[:, slot * 2048 + s * 1024: slot * 2048 + (s + 1) * 1024])
            P.stage("gp", stl, reads=[Boutb[slot]])


        P.wait_only("sp")
        P.wait_only("gp")

        nums = sorted(s.num for s in sems.values())
        assert nums == list(range(nums[0], nums[0] + len(nums))), nums
        nc.gpsimd.sem_clear(range(nums[0], nums[-1] + 1))
        nc.all_engine_barrier()

        with nc.Block() as block:
            @block.tensor
            def _(eng):
                P.run("pe", eng, sems)

            @block.scalar
            def _(eng):
                P.run("act", eng, sems)

            @block.vector
            def _(eng):
                P.run("dve", eng, sems)

            @block.sync
            def _(eng):
                P.run("sp", eng, sems)

            @block.gpsimd
            def _(eng):
                P.run("gp", eng, sems)

    return nc


_NC_CACHE = []


def kernel(L0_attn_src, L0_attn_Wq, L0_attn_bq, L0_attn_Wk, L0_attn_bk,
           L0_attn_Wv, L0_attn_bv, L0_attn_Wo, L0_attn_bo, L0_ffn_weight):
    x = np.ascontiguousarray(np.asarray(L0_attn_src))
    ws = {
        "wq": np.ascontiguousarray(np.asarray(L0_attn_Wq)),
        "wk": np.ascontiguousarray(np.asarray(L0_attn_Wk)),
        "wv": np.ascontiguousarray(np.asarray(L0_attn_Wv)),
        "wo": np.ascontiguousarray(np.asarray(L0_attn_Wo)),
        "wf": np.ascontiguousarray(np.asarray(L0_ffn_weight)),
        "bq": np.ascontiguousarray(np.asarray(L0_attn_bq)),
        "bk": np.ascontiguousarray(np.asarray(L0_attn_bk)),
        "bv": np.ascontiguousarray(np.asarray(L0_attn_bv)),
        "bo": np.ascontiguousarray(np.asarray(L0_attn_bo)),
    }
    if not _NC_CACHE:
        _NC_CACHE.append(build_program())
    nc = _NC_CACHE[0]
    in_maps = [dict(ws, x=x[c]) for c in range(8)]
    res = run_bass_kernel_spmd(nc, in_maps, core_ids=list(range(8)))
    return np.stack([np.asarray(res.results[c]["y"]) for c in range(8)], axis=0)


# version 47 (final)
# speedup vs baseline: 1.1587x; 1.0553x over previous
import numpy as np
from contextlib import ExitStack

import concourse.bass as bass
import concourse.mybir as mybir
from concourse.bass_utils import run_bass_kernel_spmd

F32 = mybir.dt.float32
BF16 = mybir.dt.bfloat16
AF = mybir.ActivationFunctionType
ALU = mybir.AluOpType
AX = mybir.AxisListType

S = 2048
D = 1024
NCH = 8
H = 8
GTOK = 256
NG = S // GTOK
NSUB = 2
SCALE = float(D // H) ** -0.5
EPS = 1e-5
ENGINES = ("pe", "act", "dve", "sp", "gp", "ra",
           "sp2", "sp3", "gp2", "gp3", "ra2", "ra3")
RING_POOL = {"sp": ("sp", "sp2", "sp3"), "gp": ("gp", "gp2", "gp3"),
             "ra": ("ra", "ra2", "ra3")}
PARANOID = False
DEBUG_DUMP = False


class Buf:
    __slots__ = ("w", "readers", "waw")

    def __init__(self, waw=False):
        self.w = None
        self.readers = {}
        self.waw = waw


class Plan:
    def __init__(self):
        self.ops = {e: [] for e in ENGINES}
        self.cnt = {e: 0 for e in ENGINES}
        self.floor = {e: {} for e in ENGINES}
        self.flushed = {e: 0 for e in ENGINES}
        self.ringn = {}

    def stage(self, eng, fn, reads=(), writes=(), inc=None, sem=None):
        if sem is None:
            sem = eng
        if sem in RING_POOL:
            pool = RING_POOL[sem]
            self.ringn.setdefault(sem, 0)
            key = sem
            sem = pool[self.ringn[key] % len(pool)]
            self.ringn[key] += 1
        if inc is None:
            inc = 16 if sem not in ("pe", "act", "dve") else 1
        waits = {}

        def need(tag, selfwait=False):
            if tag is None:
                return
            e2, v = tag
            if e2 == sem:
                if selfwait and eng in ("act", "dve") and v > self.flushed[eng]:
                    waits[eng] = max(waits.get(eng, 0), v)
                    self.flushed[eng] = v
                return
            if self.floor[eng].get(e2, 0) >= v:
                return
            waits[e2] = max(waits.get(e2, 0), v)

        if PARANOID:
            for e2 in ENGINES:
                if e2 != eng:
                    need((e2, self.cnt[e2]))
        for b in reads:
            need(b.w, selfwait=True)
        for b in writes:
            need(b.w, selfwait=b.waw)
            for e2, v in b.readers.items():
                need((e2, v), selfwait=True)
        for e2, v in waits.items():
            if e2 != sem:
                self.floor[eng][e2] = v
        if sem not in ("pe", "act", "dve") and self.cnt[sem] > 0:
            waits2 = dict(waits)
            waits2[sem] = self.cnt[sem]
            waits = waits2
        self.cnt[sem] += inc
        tag = (sem, self.cnt[sem])
        for b in reads:
            b.readers[sem] = self.cnt[sem]
        for b in writes:
            b.w = tag
            b.readers = {}
        self.ops[eng].append((sorted(waits.items()), fn, inc, sem))
        return tag

    def wait_only(self, eng):
        waits = [(e2, self.cnt[e2]) for e2 in ENGINES
                 if e2 != eng and self.cnt[e2] > self.floor[eng].get(e2, 0)]
        for e2, v in waits:
            self.floor[eng][e2] = v
        self.ops[eng].append((waits, None, 0, eng))

    def run(self, eng, handle, sems):
        for waits, fn, inc, sem in self.ops[eng]:
            for e2, v in waits:
                handle.wait_ge(sems[e2], v)
            if fn is not None:
                last = fn(handle)
                if inc:
                    last.then_inc(sems[sem], inc)


def build_program():
    nc = bass.Bass("TRN2", target_bir_lowering=False)

    x_h = nc.dram_tensor("x", [S, D], F32, kind="ExternalInput").ap()
    wq_h = nc.dram_tensor("wq", [D, D], F32, kind="ExternalInput").ap()
    wk_h = nc.dram_tensor("wk", [D, D], F32, kind="ExternalInput").ap()
    wv_h = nc.dram_tensor("wv", [D, D], F32, kind="ExternalInput").ap()
    wo_h = nc.dram_tensor("wo", [D, D], F32, kind="ExternalInput").ap()
    wf_h = nc.dram_tensor("wf", [D, D], F32, kind="ExternalInput").ap()
    bq_h = nc.dram_tensor("bq", [D], F32, kind="ExternalInput").ap()
    bk_h = nc.dram_tensor("bk", [D], F32, kind="ExternalInput").ap()
    bv_h = nc.dram_tensor("bv", [D], F32, kind="ExternalInput").ap()
    bo_h = nc.dram_tensor("bo", [D], F32, kind="ExternalInput").ap()
    y_h = nc.dram_tensor("y", [S, D], F32, kind="ExternalOutput").ap()
    dbg_h = {}
    if DEBUG_DUMP:
        for nm, dt_ in (("x_tm", F32), ("x_bf", BF16), ("x_fm", BF16),
                        ("q_fm", BF16), ("k_fm", BF16), ("v_tm", BF16),
                        ("expb", F32), ("probsn", BF16), ("probsT", BF16),
                        ("ao", BF16), ("r1", F32), ("ln1", F32),
                        ("ln1bf", BF16), ("ln1fm", BF16),
                        ("ssum", F32), ("rden", F32)):
            w = 4096 if nm in ("x_tm", "x_bf") else (16 if nm in ("ssum", "rden") else 2048)
            dbg_h[nm] = nc.dram_tensor("dbg_" + nm, [128, w], dt_,
                                       kind="ExternalOutput").ap()

    with ExitStack() as ctx:
        sb = lambda name, shape, dt: ctx.enter_context(nc.sbuf_tensor(name, shape, dt))
        ps = lambda name, shape: ctx.enter_context(nc.psum_tensor(name, shape, F32))

        wq = sb("wq_sb", [128, NCH * D], BF16)
        wk = sb("wk_sb", [128, NCH * D], BF16)
        wv = sb("wv_sb", [128, NCH * D], BF16)
        wo = sb("wo_sb", [128, NCH * D], BF16)
        swt = sb("swt_sb", [128, NCH * D], BF16)
        sws = sb("sws_sb", [128, NCH * D], BF16)
        brt = sb("brt", [128, 128], F32)
        bcol = sb("bcol", [128, 128], F32)
        bvc = sb("bv_bf", [128, NCH], BF16)
        borow = sb("bo_row", [1, D], F32)
        boeff = sb("boeff", [1, D], BF16)
        onesb = sb("ones_bf", [1, 256], BF16)

        x_tm = sb("x_tm", [128, 4096], F32)
        x_bf = sb("x_bf", [128, 4096], BF16)
        x_fm = sb("x_fm", [128, 4096], BF16)
        q_fm = sb("q_fm", [128, 2048], BF16)
        k_fm = sb("k_fm", [128, 2048], BF16)
        v_tm = sb("v_tm", [128, 2048], BF16)
        probsn = sb("probsn", [128, 2048], BF16)
        probsT = sb("probsT", [128, 2048], BF16)
        ao = sb("ao_sb", [128, 2048], BF16)
        r1 = sb("r1", [128, 2048], F32)
        ln1 = sb("ln1", [128, 2048], F32)
        ln1bf = sb("ln1bf", [128, 2048], BF16)
        ln1fm = sb("ln1fm", [128, 2048], BF16)
        outb = sb("outb", [128, 4096], F32)
        sqs = sb("sqs", [128, 1024], BF16)
        epsc = sb("epsc", [128, 1], F32)

        ssum = sb("ssum", [128, 2 * H], F32)
        rden = sb("rden", [128, 2 * H], F32)
        rsum1 = sb("rsum1", [128, 4], F32)
        ssq1 = sb("ssq1", [128, 2], F32)
        msum1 = sb("msum1", [128, 2], F32)
        msq1 = sb("msq1", [128, 4], F32)
        varr1 = sb("varr1", [128, 2], F32)
        lnv1 = sb("lnv1", [128, 2], F32)
        istd1 = sb("istd1", [128, 2], F32)
        ngm1 = sb("ngm1", [128, 2], F32)
        rsum2 = sb("rsum2", [128, 4], F32)
        ssq2 = sb("ssq2", [128, 2], F32)
        msum2 = sb("msum2", [128, 2], F32)
        msq2 = sb("msq2", [128, 4], F32)
        varr2 = sb("varr2", [128, 2], F32)
        lnv2 = sb("lnv2", [128, 2], F32)
        istd2 = sb("istd2", [128, 2], F32)
        ngm2 = sb("ngm2", [128, 2], F32)

        pb = [ps(f"pb{i}", [128, 512]) for i in range(6)]
        psc = ps("psc", [128, 1024])

        sems = {e: ctx.enter_context(nc.semaphore(f"sem_{e}")) for e in ENGINES}

        P = Plan()
        Bwq, Bwk, Bwv, Bwo = (Buf() for _ in range(4))
        Bsws = [Buf() for _ in range(NCH)]
        Bsg = [Buf() for _ in range(NCH)]
        Bswt = [Buf() for _ in range(NCH)]
        Bbq, Bbk, Bbv, Bborow, Bboeff, Bones = (Buf() for _ in range(6))
        Bbrt, Bbcol, Bbvc = Buf(), Buf(), Buf()
        Beps = Buf()
        Bxtm = [Buf(), Buf()]
        Bxbf = [Buf(), Buf()]
        Boutb = [Buf(), Buf()]
        Bqfm, Bkfm = Buf(), Buf()
        Bln1fm = [Buf(), Buf()]
        Bsqs = Buf(waw=True)
        Bxfm = [[Buf(), Buf()], [Buf(), Buf()]]
        Bvtm = [Buf(), Buf()]
        Bprobsn = [Buf(), Buf()]
        BprobsT = [Buf(), Buf()]
        Bao = [Buf(), Buf()]
        Br = [Buf(), Buf()]
        Bln1 = [Buf(), Buf()]
        Bln1bf = [Buf(), Buf()]
        Bssum, Brden = Buf(), Buf()
        Brs1, Bsq1, Bms1, Bmq1, Bvr1, Blv1, Bis1, Bng1 = (Buf() for _ in range(8))
        Brs2, Bsq2, Bms2, Bmq2, Bvr2, Blv2, Bis2, Bng2 = (Buf() for _ in range(8))
        Bpb = [Buf() for _ in range(6)]
        Bpsc = Buf()

        rot = [0, 0]

        def next_bank(pool):
            if pool == 0:
                i = rot[0] % 3
            else:
                i = 3 + rot[1] % 3
            rot[pool] += 1
            return pb[i], Bpb[i]

        def ld_x(g, slot, out_t, cast):
            def fn(e, g=g, slot=slot, out_t=out_t):
                return e.dma_start(
                    out_t[:, slot * 2048:(slot + 1) * 2048].rearrange(
                        "p (s d) -> p s d", s=2),
                    x_h[g * GTOK:(g + 1) * GTOK, :].rearrange(
                        "(s p) d -> p s d", p=128),
                )
            return fn

        def ld_w(wt, wh, Bw):
            def fn(e, wt=wt, wh=wh):
                return e.dma_start(
                    wt[:, :].rearrange("p (k o) -> p k o", k=8),
                    wh.rearrange("(k p) o -> p k o", p=128))
            P.stage("gp", fn, writes=[Bw])

        def ld_b(bh, base, Bb):
            def fn(e, bh=bh, base=base):
                return e.dma_start(
                    brt[base:base + 8, :], bh.rearrange("(p f) -> p f", p=8))
            P.stage("sp", fn, reads=[Bbrt], writes=[Bb])

        P.stage("gp", ld_x(0, 0, x_bf, True), writes=[Bxbf[0]])
        ld_w(wq, wq_h, Bwq)
        P.stage("dve", lambda e: e.memset(brt[:, :], 0.0), writes=[Bbrt])
        ld_b(bq_h, 0, Bbq)
        ld_b(bk_h, 32, Bbk)
        ld_b(bv_h, 64, Bbv)
        for rb, Bb in ((0, Bbq), (32, Bbk), (64, Bbv)):
            for J in range(4):
                P.stage("dve", lambda e, rb=rb, J=J: e.transpose(
                    bcol[32 * J:32 * J + 32, rb:rb + 32],
                    brt[rb:rb + 32, 32 * J:32 * J + 32]),
                    reads=[Bb, Bbrt], writes=[Bbcol])
        P.stage("dve", lambda e: e.tensor_copy(bvc[:, :], bcol[:, 64:64 + NCH]),
                reads=[Bbcol], writes=[Bbvc])
        ld_w(wk, wk_h, Bwk)
        P.stage("gp", lambda e: e.dma_start(
            borow[:, :], bo_h.rearrange("(one d) -> one d", one=1)), writes=[Bborow])
        ld_w(wv, wv_h, Bwv)
        P.stage("gp", ld_x(0, 0, x_tm, False), writes=[Bxtm[0]])
        ld_w(wo, wo_h, Bwo)

        P.stage("dve", lambda e: e.memset(onesb[:, :], 1.0), writes=[Bones])
        P.stage("dve", lambda e: e.memset(epsc[:, :], EPS), writes=[Beps])
        P.stage("dve", lambda e: e.memset(probsn[:, :], 0.0),
                writes=[Bprobsn[0], Bprobsn[1]])

        for oc in range(NCH):
            P.stage("gp", lambda e, oc=oc: e.dma_start(
                sws[:, oc * D:(oc + 1) * D],
                wf_h[oc * 128:(oc + 1) * 128, :]), writes=[Bsws[oc]])

        def emit_sws_sign(lo, hi):
            for oc in range(lo, hi):
                P.stage("act", lambda e, oc=oc: e.activation(
                    sws[:, oc * D:(oc + 1) * D], sws[:, oc * D:(oc + 1) * D],
                    AF.Sign),
                    reads=[Bsws[oc]], writes=[Bsg[oc]])

        def emit_sws_trans():
            for oc in range(NCH):
                def fn(e, oc=oc):
                    return e.dma_start(
                        bass.AP(swt, oc * 128, [[NCH * D, 128], [D, 8], [1, 128]]),
                        sws[:, oc * D:(oc + 1) * D].rearrange(
                            "p (c i) -> p c i", c=8),
                        transpose=True)
                P.stage("sp", fn, reads=[Bsg[oc]], writes=[Bswt[oc]])

        def emit_loads(g):
            if g >= NG:
                return
            ns = g % 2
            P.stage("gp", ld_x(g, ns, x_bf, True), writes=[Bxbf[ns]])
            P.stage("gp", ld_x(g, ns, x_tm, False), writes=[Bxtm[ns]])

        def emit_xfm(g):
            if g >= NG:
                return
            slot = g % 2
            for s in range(NSUB):
                def fn(e, s=s, slot=slot):
                    return e.dma_start(
                        bass.AP(x_fm, slot * 2048 + s * 128,
                                [[4096, 128], [256, 8], [1, 128]]),
                        x_bf[:, slot * 2048 + s * 1024: slot * 2048 + (s + 1) * 1024
                             ].rearrange("p (c f) -> p c f", c=8),
                        transpose=True)
                P.stage("sp", fn, reads=[Bxbf[slot]], writes=[Bxfm[slot][s]])

        def emit_qk(g, which):
            if g >= NG:
                return
            sel = ((wq, Bwq, 0, q_fm, Bqfm),
                   (wk, Bwk, 32, k_fm, Bkfm))
            for (wt, Bw, boff, dst, Bdst) in sel[which:which + 1]:
                for m2 in range(NCH // 2):
                    bank, Bbank = next_bank(0)
                    slot = g % 2
                    for mi in range(2):
                        m = m2 * 2 + mi
                        def mmfn(e, wt=wt, m=m, mi=mi, bank=bank, slot=slot):
                            last = None
                            for kk in range(NCH):
                                last = e.matmul(
                                    bank[:, mi * GTOK:(mi + 1) * GTOK],
                                    wt[:, kk * D + m * 128: kk * D + (m + 1) * 128],
                                    x_fm[:, slot * 2048 + kk * GTOK:
                                         slot * 2048 + (kk + 1) * GTOK],
                                    start=(kk == 0), stop=(kk == NCH - 1))
                            return last
                        P.stage("pe", mmfn,
                                reads=[Bw, Bxfm[slot][0], Bxfm[slot][1]],
                                writes=[Bbank])
                    for mi in range(2):
                        m = m2 * 2 + mi
                        def evfn(e, m=m, mi=mi, bank=bank, boff=boff, dst=dst):
                            return e.activation(
                                dst[:, m * GTOK:(m + 1) * GTOK],
                                bank[:, mi * GTOK:(mi + 1) * GTOK],
                                AF.Identity,
                                bias=bcol[:, boff + m:boff + m + 1], scale=1.0)
                        P.stage("act", evfn, reads=[Bbank, Bbcol], writes=[Bdst])

        def emit_v(g):
            for s in range(NSUB):
                for hf in range(2):
                    bank, Bbank = next_bank(0)
                    slot = g % 2
                    def mmfn(e, s=s, hf=hf, bank=bank, slot=slot):
                        last = None
                        for kk in range(NCH):
                            last = e.matmul(
                                bank[:, 0:512],
                                x_fm[:, slot * 2048 + kk * GTOK + s * 128:
                                     slot * 2048 + kk * GTOK + (s + 1) * 128],
                                wv[:, kk * D + hf * 512: kk * D + (hf + 1) * 512],
                                start=(kk == 0), stop=(kk == NCH - 1))
                        return last
                    P.stage("pe", mmfn, reads=[Bwv, Bxfm[slot][s]], writes=[Bbank])

                    def evfn(e, s=s, hf=hf, bank=bank):
                        return e.copy(
                            v_tm[:, s * 1024 + hf * 512: s * 1024 + (hf + 1) * 512],
                            bank[:, 0:512])
                    P.stage("act", evfn, reads=[Bbank], writes=[Bvtm[s]])

        def emit_boeff():
            for hf in range(2):
                bank, Bbank = next_bank(1)
                def mmfn(e, hf=hf, bank=bank):
                    last = None
                    for kk in range(NCH):
                        last = e.matmul(
                            bank[0:1, 0:512],
                            bvc[:, kk:kk + 1],
                            wo[:, kk * D + hf * 512: kk * D + (hf + 1) * 512],
                            start=(kk == 0), stop=(kk == NCH - 1))
                    return last
                P.stage("pe", mmfn, reads=[Bbvc, Bwo], writes=[Bbank])

                def ttfn(e, hf=hf, bank=bank):
                    return e.tensor_tensor(
                        boeff[0:1, hf * 512:(hf + 1) * 512],
                        bank[0:1, 0:512],
                        borow[0:1, hf * 512:(hf + 1) * 512], ALU.add)
                P.stage("dve", ttfn, reads=[Bbank, Bborow], writes=[Bboeff])

        def emit_scores_s(g, s):
            def scfn(e, s=s):
                last = None
                for h in range(H):
                    last = e.matmul(
                        psc[:, h * 128:(h + 1) * 128],
                        q_fm[:, h * GTOK + s * 128: h * GTOK + (s + 1) * 128],
                        k_fm[:, h * GTOK + s * 128: h * GTOK + (s + 1) * 128],
                        start=True, stop=True)
                return last
            P.stage("pe", scfn, reads=[Bqfm, Bkfm], writes=[Bpsc])

        def emit_exp_s(g, s):
            for half in range(2):
                def expfn(e, s=s, half=half):
                    q0 = 64 * half
                    return e.activation(
                        probsn[q0:q0 + 64, s * 1024:(s + 1) * 1024].rearrange(
                            "p (h k) -> p h k", h=H)[:, :, q0:q0 + 64],
                        psc[q0:q0 + 64, :].rearrange(
                            "p (h k) -> p h k", h=H)[:, :, q0:q0 + 64],
                        AF.Exp, scale=SCALE)
                P.stage("act", expfn, reads=[Bpsc], writes=[Bprobsn[s]])

        def emit_softmax_s(g, s):
            def sumfn(e, s=s):
                return e.reduce_sum(
                    ssum[:, s * H:(s + 1) * H],
                    probsn[:, s * 1024:(s + 1) * 1024].rearrange(
                        "p (h k) -> p h k", h=H),
                    axis=AX.X)
            P.stage("dve", sumfn, reads=[Bprobsn[s]], writes=[Bssum])
            P.stage("dve", lambda e, s=s: e.reciprocal(
                rden[:, s * H:(s + 1) * H], ssum[:, s * H:(s + 1) * H]),
                reads=[Bssum], writes=[Brden])

            def nrmfn(e, s=s):
                return e.scalar_tensor_tensor(
                    probsn[:, s * 1024:(s + 1) * 1024],
                    probsn[:, s * 1024:(s + 1) * 1024], 1.0,
                    bass.AP(rden, s * H, [[2 * H, 128], [1, 8], [0, 128]]),
                    op0=ALU.mult, op1=ALU.mult)
            P.stage("dve", nrmfn, reads=[Bprobsn[s], Brden], writes=[Bprobsn[s]])

            def ptfn(e, s=s):
                return e.dma_start(
                    probsT[:, s * 1024:(s + 1) * 1024].rearrange(
                        "p (h q) -> p h q", h=H),
                    probsn[:, s * 1024:(s + 1) * 1024].rearrange(
                        "p (h k) -> p h k", h=H),
                    transpose=True)
            P.stage("sp", ptfn, reads=[Bprobsn[s]], writes=[BprobsT[s]])

        def emit_attnout(g):
            for s in range(NSUB):
                for half in range(2):
                    bank, Bbank = next_bank(1)
                    def aofn(e, s=s, half=half, bank=bank):
                        last = None
                        for hh in range(4):
                            h = half * 4 + hh
                            last = e.matmul(
                                bank[:, hh * 128:(hh + 1) * 128],
                                v_tm[:, s * 1024 + h * 128: s * 1024 + (h + 1) * 128],
                                probsT[:, s * 1024 + h * 128: s * 1024 + (h + 1) * 128],
                                start=True, stop=True)
                        return last
                    P.stage("pe", aofn, reads=[Bvtm[s], BprobsT[s]], writes=[Bbank])

                    def aoev(e, s=s, half=half, bank=bank):
                        return e.copy(
                            ao[:, s * 1024 + half * 512: s * 1024 + (half + 1) * 512],
                            bank[:, 0:512])
                    P.stage("act", aoev, reads=[Bbank], writes=[Bao[s]])

        def emit_oproj_s(g, s):
            slot = g % 2
            if True:
                for hf in range(2):
                    bank, Bbank = next_bank(1)
                    def opfn(e, s=s, hf=hf, bank=bank):
                        for kk in range(NCH):
                            e.matmul(
                                bank[:, 0:512],
                                ao[:, s * 1024 + kk * 128: s * 1024 + (kk + 1) * 128],
                                wo[:, kk * D + hf * 512: kk * D + (hf + 1) * 512],
                                start=(kk == 0), stop=False)
                        return e.matmul(
                            bank[:, 0:512], onesb[0:1, 0:128],
                            boeff[0:1, hf * 512:(hf + 1) * 512],
                            start=False, stop=True)
                    P.stage("pe", opfn, reads=[Bao[s], Bwo, Bones, Bboeff],
                            writes=[Bbank])

                    def r1fn(e, s=s, hf=hf, bank=bank, slot=slot):
                        off = s * 1024 + hf * 512
                        return e.scalar_tensor_tensor(
                            r1[:, off:off + 512], bank[:, 0:512], 1.0,
                            x_tm[:, slot * 2048 + off: slot * 2048 + off + 512],
                            op0=ALU.mult, op1=ALU.add,
                            accum_out=rsum1[:, s * 2 + hf: s * 2 + hf + 1])
                    P.stage("dve", r1fn, reads=[Bbank, Bxtm[slot]],
                            writes=[Br[s], Brs1])

        def emit_oproj(g):
            for s in range(NSUB):
                emit_oproj_s(g, s)

        def emit_ln1(g):
            for s in range(NSUB):
                def sqfn(e, s=s):
                    return e.scalar_tensor_tensor(
                        sqs[:, :], r1[:, s * 1024:(s + 1) * 1024], 1.0,
                        r1[:, s * 1024:(s + 1) * 1024],
                        op0=ALU.mult, op1=ALU.mult,
                        accum_out=ssq1[:, s:s + 1])
                P.stage("dve", sqfn, reads=[Br[s]], writes=[Bsqs, Bsq1])
            P.stage("dve", lambda e: e.tensor_tensor(
                msum1[:, 0:2], bass.AP(rsum1, 0, [[4, 128], [2, 2]]),
                bass.AP(rsum1, 1, [[4, 128], [2, 2]]), ALU.add),
                reads=[Brs1], writes=[Bms1])
            P.stage("dve", lambda e: e.tensor_tensor(
                msq1[:, 0:2], msum1[:, 0:2], msum1[:, 0:2], ALU.mult),
                reads=[Bms1], writes=[Bmq1])
            P.stage("dve", lambda e: e.scalar_tensor_tensor(
                varr1[:, 0:2], msq1[:, 0:2], 1.0 / D, ssq1[:, 0:2],
                op0=ALU.mult, op1=ALU.subtract),
                reads=[Bsq1, Bmq1], writes=[Bvr1])
            P.stage("act", lambda e: e.activation(
                lnv1[:, 0:2], varr1[:, 0:2], AF.Ln,
                bias=epsc[:, 0:1], scale=-1.0 / D),
                reads=[Bvr1, Beps], writes=[Blv1])
            P.stage("act", lambda e: e.activation(
                istd1[:, 0:2], lnv1[:, 0:2], AF.Exp, scale=-0.5),
                reads=[Blv1], writes=[Bis1])
            P.stage("dve", lambda e: e.tensor_scalar(
                msq1[:, 2:4], msum1[:, 0:2], -1.0 / D, None, op0=ALU.mult),
                reads=[Bms1], writes=[Bmq1])
            P.stage("dve", lambda e: e.tensor_tensor(
                ngm1[:, 0:2], istd1[:, 0:2], msq1[:, 2:4], ALU.mult),
                reads=[Bis1, Bmq1], writes=[Bng1])
            for s in range(NSUB):
                P.stage("dve", lambda e, s=s: e.tensor_scalar(
                    ln1[:, s * 1024:(s + 1) * 1024],
                    r1[:, s * 1024:(s + 1) * 1024],
                    istd1[:, s:s + 1], ngm1[:, s:s + 1],
                    op0=ALU.mult, op1=ALU.add),
                    reads=[Br[s], Bis1, Bng1], writes=[Bln1[s]])
                P.stage("act", lambda e, s=s: e.activation(
                    ln1bf[:, s * 1024:(s + 1) * 1024],
                    r1[:, s * 1024:(s + 1) * 1024],
                    AF.Identity, bias=ngm1[:, s:s + 1],
                    scale=istd1[:, s:s + 1]),
                    reads=[Br[s], Bis1, Bng1], writes=[Bln1bf[s]])

                def lfmfn(e, s=s):
                    return e.dma_start(
                        bass.AP(ln1fm, s * 128, [[2048, 128], [256, 8], [1, 128]]),
                        ln1bf[:, s * 1024:(s + 1) * 1024].rearrange(
                            "p (c f) -> p c f", c=8),
                        transpose=True)
                P.stage("sp", lfmfn, reads=[Bln1bf[s]], writes=[Bln1fm[s]])

        def emit_ffn_s(g, s):
            if True:
                for hf in range(2):
                    bank, Bbank = next_bank(1)
                    def fffn(e, s=s, hf=hf, bank=bank):
                        last = None
                        for kk in range(NCH):
                            last = e.matmul(
                                bank[:, 0:512],
                                ln1fm[:, kk * GTOK + s * 128: kk * GTOK + (s + 1) * 128],
                                swt[:, kk * D + hf * 512: kk * D + (hf + 1) * 512],
                                start=(kk == 0), stop=(kk == NCH - 1))
                        return last
                    P.stage("pe", fffn, reads=[Bln1fm[s]] + Bswt, writes=[Bbank])

                    def r2fn(e, s=s, hf=hf, bank=bank):
                        off = s * 1024 + hf * 512
                        return e.scalar_tensor_tensor(
                            r1[:, off:off + 512], bank[:, 0:512], 1.0,
                            ln1[:, off:off + 512],
                            op0=ALU.mult, op1=ALU.add,
                            accum_out=rsum2[:, s * 2 + hf: s * 2 + hf + 1])
                    P.stage("dve", r2fn, reads=[Bbank, Bln1[s]],
                            writes=[Br[s], Brs2])

        def emit_ffn(g):
            for s in range(NSUB):
                emit_ffn_s(g, s)

        def emit_ln1_s(g, s):
            def sqfn(e, s=s):
                return e.scalar_tensor_tensor(
                    sqs[:, :], r1[:, s * 1024:(s + 1) * 1024], 1.0,
                    r1[:, s * 1024:(s + 1) * 1024],
                    op0=ALU.mult, op1=ALU.mult,
                    accum_out=ssq1[:, s:s + 1])
            P.stage("dve", sqfn, reads=[Br[s]], writes=[Bsqs, Bsq1])
            P.stage("dve", lambda e, s=s: e.tensor_tensor(
                msum1[:, s:s + 1], rsum1[:, 2 * s:2 * s + 1],
                rsum1[:, 2 * s + 1:2 * s + 2], ALU.add),
                reads=[Brs1], writes=[Bms1])
            P.stage("dve", lambda e, s=s: e.tensor_tensor(
                msq1[:, s:s + 1], msum1[:, s:s + 1], msum1[:, s:s + 1],
                ALU.mult),
                reads=[Bms1], writes=[Bmq1])
            P.stage("dve", lambda e, s=s: e.scalar_tensor_tensor(
                varr1[:, s:s + 1], msq1[:, s:s + 1], 1.0 / D, ssq1[:, s:s + 1],
                op0=ALU.mult, op1=ALU.subtract),
                reads=[Bsq1, Bmq1], writes=[Bvr1])
            P.stage("act", lambda e, s=s: e.activation(
                lnv1[:, s:s + 1], varr1[:, s:s + 1], AF.Ln,
                bias=epsc[:, 0:1], scale=-1.0 / D),
                reads=[Bvr1, Beps], writes=[Blv1])
            P.stage("act", lambda e, s=s: e.activation(
                istd1[:, s:s + 1], lnv1[:, s:s + 1], AF.Exp, scale=-0.5),
                reads=[Blv1], writes=[Bis1])
            P.stage("dve", lambda e, s=s: e.tensor_scalar(
                msq1[:, 2 + s:3 + s], msum1[:, s:s + 1], -1.0 / D, None,
                op0=ALU.mult),
                reads=[Bms1], writes=[Bmq1])
            P.stage("dve", lambda e, s=s: e.tensor_tensor(
                ngm1[:, s:s + 1], istd1[:, s:s + 1], msq1[:, 2 + s:3 + s],
                ALU.mult),
                reads=[Bis1, Bmq1], writes=[Bng1])
            P.stage("dve", lambda e, s=s: e.tensor_scalar(
                ln1[:, s * 1024:(s + 1) * 1024],
                r1[:, s * 1024:(s + 1) * 1024],
                istd1[:, s:s + 1], ngm1[:, s:s + 1],
                op0=ALU.mult, op1=ALU.add),
                reads=[Br[s], Bis1, Bng1], writes=[Bln1[s]])
            P.stage("act", lambda e, s=s: e.activation(
                ln1bf[:, s * 1024:(s + 1) * 1024],
                r1[:, s * 1024:(s + 1) * 1024],
                AF.Identity, bias=ngm1[:, s:s + 1],
                scale=istd1[:, s:s + 1]),
                reads=[Br[s], Bis1, Bng1], writes=[Bln1bf[s]])

            def lfmfn(e, s=s):
                return e.dma_start(
                    bass.AP(ln1fm, s * 128, [[2048, 128], [256, 8], [1, 128]]),
                    ln1bf[:, s * 1024:(s + 1) * 1024].rearrange(
                        "p (c f) -> p c f", c=8),
                    transpose=True)
            P.stage("sp", lfmfn, reads=[Bln1bf[s]], writes=[Bln1fm[s]])

        def emit_ln2(g):
            slot = g % 2
            for s in range(NSUB):
                def sq2fn(e, s=s):
                    return e.scalar_tensor_tensor(
                        sqs[:, :], r1[:, s * 1024:(s + 1) * 1024], 1.0,
                        r1[:, s * 1024:(s + 1) * 1024],
                        op0=ALU.mult, op1=ALU.mult,
                        accum_out=ssq2[:, s:s + 1])
                P.stage("dve", sq2fn, reads=[Br[s]], writes=[Bsqs, Bsq2])
            P.stage("dve", lambda e: e.tensor_tensor(
                msum2[:, 0:2], bass.AP(rsum2, 0, [[4, 128], [2, 2]]),
                bass.AP(rsum2, 1, [[4, 128], [2, 2]]), ALU.add),
                reads=[Brs2], writes=[Bms2])
            P.stage("dve", lambda e: e.tensor_tensor(
                msq2[:, 0:2], msum2[:, 0:2], msum2[:, 0:2], ALU.mult),
                reads=[Bms2], writes=[Bmq2])
            P.stage("dve", lambda e: e.scalar_tensor_tensor(
                varr2[:, 0:2], msq2[:, 0:2], 1.0 / D, ssq2[:, 0:2],
                op0=ALU.mult, op1=ALU.subtract),
                reads=[Bsq2, Bmq2], writes=[Bvr2])
            P.stage("act", lambda e: e.activation(
                lnv2[:, 0:2], varr2[:, 0:2], AF.Ln,
                bias=epsc[:, 0:1], scale=-1.0 / D),
                reads=[Bvr2, Beps], writes=[Blv2])
            P.stage("act", lambda e: e.activation(
                istd2[:, 0:2], lnv2[:, 0:2], AF.Exp, scale=-0.5),
                reads=[Blv2], writes=[Bis2])
            P.stage("dve", lambda e: e.tensor_scalar(
                msq2[:, 2:4], msum2[:, 0:2], -1.0 / D, None, op0=ALU.mult),
                reads=[Bms2], writes=[Bmq2])
            P.stage("dve", lambda e: e.tensor_tensor(
                ngm2[:, 0:2], istd2[:, 0:2], msq2[:, 2:4], ALU.mult),
                reads=[Bis2, Bmq2], writes=[Bng2])
            for s in range(NSUB):
                P.stage("dve", lambda e, s=s, slot=slot: e.tensor_scalar(
                    outb[:, slot * 2048 + s * 1024: slot * 2048 + (s + 1) * 1024],
                    r1[:, s * 1024:(s + 1) * 1024],
                    istd2[:, s:s + 1], ngm2[:, s:s + 1],
                    op0=ALU.mult, op1=ALU.add),
                    reads=[Br[s], Bis2, Bng2], writes=[Boutb[slot]])

        def emit_store(g):
            slot = g % 2
            def stfn2(e, g=g, slot=slot):
                return e.dma_start(
                    y_h[g * GTOK:(g + 1) * GTOK, :].rearrange(
                        "(s p) d -> p s d", p=128),
                    outb[:, slot * 2048:(slot + 1) * 2048].rearrange(
                        "p (s d) -> p s d", s=2))
            P.stage("gp", stfn2, reads=[Boutb[slot]])

        emit_xfm(0)
        emit_qk(0, 0)
        emit_qk(0, 1)
        emit_scores_s(0, 0)
        emit_exp_s(0, 0)
        emit_softmax_s(0, 0)
        emit_scores_s(0, 1)
        emit_exp_s(0, 1)
        emit_softmax_s(0, 1)
        for g in range(NG):
            emit_loads(g + 1)
            emit_xfm(g + 1)
            emit_v(g)
            if g == 0:
                emit_boeff()
            emit_attnout(g)
            if g > 0:
                emit_ln2(g - 1)
                emit_store(g - 1)
            if g == NG - 1:
                emit_oproj_s(g, 0)
                emit_ln1_s(g, 0)
                emit_oproj_s(g, 1)
                emit_ffn_s(g, 0)
                emit_ln1_s(g, 1)
                emit_ffn_s(g, 1)
                continue
            emit_oproj(g)
            if g == 0:
                emit_sws_sign(0, 4)
            emit_qk(g + 1, 0)
            if g == 0:
                emit_sws_sign(4, NCH)
            emit_ln1(g)
            emit_qk(g + 1, 1)
            if g == 0:
                emit_sws_trans()
            emit_ffn_s(g, 0)
            emit_scores_s(g + 1, 0)
            emit_exp_s(g + 1, 0)
            emit_softmax_s(g + 1, 0)
            emit_ffn_s(g, 1)
            emit_scores_s(g + 1, 1)
            emit_exp_s(g + 1, 1)
            emit_softmax_s(g + 1, 1)
        slot = (NG - 1) % 2
        for s in range(NSUB):
            def sqln(e, s=s):
                return e.scalar_tensor_tensor(
                    sqs[:, :], r1[:, s * 1024:(s + 1) * 1024], 1.0,
                    r1[:, s * 1024:(s + 1) * 1024],
                    op0=ALU.mult, op1=ALU.mult,
                    accum_out=ssq2[:, s:s + 1])
            P.stage("dve", sqln, reads=[Br[s]], writes=[Bsqs, Bsq2])
            P.stage("dve", lambda e, s=s: e.tensor_tensor(
                msum2[:, s:s + 1], rsum2[:, 2 * s:2 * s + 1],
                rsum2[:, 2 * s + 1:2 * s + 2], ALU.add),
                reads=[Brs2], writes=[Bms2])
            P.stage("dve", lambda e, s=s: e.tensor_tensor(
                msq2[:, s:s + 1], msum2[:, s:s + 1], msum2[:, s:s + 1],
                ALU.mult),
                reads=[Bms2], writes=[Bmq2])
            P.stage("dve", lambda e, s=s: e.scalar_tensor_tensor(
                varr2[:, s:s + 1], msq2[:, s:s + 1], 1.0 / D, ssq2[:, s:s + 1],
                op0=ALU.mult, op1=ALU.subtract),
                reads=[Bsq2, Bmq2], writes=[Bvr2])
            P.stage("act", lambda e, s=s: e.activation(
                lnv2[:, s:s + 1], varr2[:, s:s + 1], AF.Ln,
                bias=epsc[:, 0:1], scale=-1.0 / D),
                reads=[Bvr2, Beps], writes=[Blv2])
            P.stage("act", lambda e, s=s: e.activation(
                istd2[:, s:s + 1], lnv2[:, s:s + 1], AF.Exp, scale=-0.5),
                reads=[Blv2], writes=[Bis2])
            P.stage("dve", lambda e, s=s: e.tensor_scalar(
                msq2[:, 2 + s:3 + s], msum2[:, s:s + 1], -1.0 / D, None,
                op0=ALU.mult),
                reads=[Bms2], writes=[Bmq2])
            P.stage("dve", lambda e, s=s: e.tensor_tensor(
                ngm2[:, s:s + 1], istd2[:, s:s + 1], msq2[:, 2 + s:3 + s],
                ALU.mult),
                reads=[Bis2, Bmq2], writes=[Bng2])
            P.stage("dve", lambda e, s=s, slot=slot: e.tensor_scalar(
                outb[:, slot * 2048 + s * 1024: slot * 2048 + (s + 1) * 1024],
                r1[:, s * 1024:(s + 1) * 1024],
                istd2[:, s:s + 1], ngm2[:, s:s + 1],
                op0=ALU.mult, op1=ALU.add),
                reads=[Br[s], Bis2, Bng2], writes=[Boutb[slot]])

            def stl(e, s=s, slot=slot):
                g = NG - 1
                return e.dma_start(
                    y_h[g * GTOK + s * 128: g * GTOK + (s + 1) * 128, :],
                    outb[:, slot * 2048 + s * 1024: slot * 2048 + (s + 1) * 1024])
            P.stage("gp", stl, reads=[Boutb[slot]])


        P.wait_only("sp")
        P.wait_only("gp")

        nums = sorted(s.num for s in sems.values())
        assert nums == list(range(nums[0], nums[0] + len(nums))), nums
        nc.gpsimd.sem_clear(range(nums[0], nums[-1] + 1))
        nc.all_engine_barrier()

        with nc.Block() as block:
            @block.tensor
            def _(eng):
                P.run("pe", eng, sems)

            @block.scalar
            def _(eng):
                P.run("act", eng, sems)

            @block.vector
            def _(eng):
                P.run("dve", eng, sems)

            @block.sync
            def _(eng):
                P.run("sp", eng, sems)

            @block.gpsimd
            def _(eng):
                P.run("gp", eng, sems)

    return nc


_NC_CACHE = []


def kernel(L0_attn_src, L0_attn_Wq, L0_attn_bq, L0_attn_Wk, L0_attn_bk,
           L0_attn_Wv, L0_attn_bv, L0_attn_Wo, L0_attn_bo, L0_ffn_weight):
    x = np.ascontiguousarray(np.asarray(L0_attn_src))
    ws = {
        "wq": np.ascontiguousarray(np.asarray(L0_attn_Wq)),
        "wk": np.ascontiguousarray(np.asarray(L0_attn_Wk)),
        "wv": np.ascontiguousarray(np.asarray(L0_attn_Wv)),
        "wo": np.ascontiguousarray(np.asarray(L0_attn_Wo)),
        "wf": np.ascontiguousarray(np.asarray(L0_ffn_weight)),
        "bq": np.ascontiguousarray(np.asarray(L0_attn_bq)),
        "bk": np.ascontiguousarray(np.asarray(L0_attn_bk)),
        "bv": np.ascontiguousarray(np.asarray(L0_attn_bv)),
        "bo": np.ascontiguousarray(np.asarray(L0_attn_bo)),
    }
    if not _NC_CACHE:
        _NC_CACHE.append(build_program())
    nc = _NC_CACHE[0]
    in_maps = [dict(ws, x=x[c]) for c in range(8)]
    res = run_bass_kernel_spmd(nc, in_maps, core_ids=list(range(8)))
    return np.stack([np.asarray(res.results[c]["y"]) for c in range(8)], axis=0)
